# version 25
# baseline (speedup 1.0000x reference)
import contextlib
import math

import numpy as np
import concourse.bass as bass
import concourse.mybir as mybir
from concourse.bass_utils import run_bass_kernel_spmd

F32 = mybir.dt.float32
FP8 = mybir.dt.float8e4
BF16 = mybir.dt.bfloat16
ALU = mybir.AluOpType
ACT = mybir.ActivationFunctionType
AXX = mybir.AxisListType.X
LN2 = float(np.log(2.0))

ENGINES = ("sync", "scalar", "vector", "tensor", "gpsimd")

B, SEQ, D = 16, 4096, 1024
NSEQ = 2
HFF = 4096
C1 = 128
NFC = 8
NHC = 32
B32 = float(np.float32(1.5 * 2.0 ** 23))
B64 = float(np.float32(1.5 * 2.0 ** 52))


NPOOL = 8


class Sched:

    def __init__(self, nc, stack):
        self.nc = nc
        self.ops = {e: [] for e in ENGINES}
        self.serial = {}
        self.writes = {}
        self.readers = {}
        self.sems = {}
        for e in ENGINES:
            n = f"{e}_x"
            self.sems[n] = stack.enter_context(nc.semaphore(n))
            self.serial[n] = 0
        self.dmaq = {}
        for e in ("sync", "gpsimd", "scalar", "vector"):
            pool = []
            for j in range(NPOOL):
                n = f"{e}_d{j}"
                self.sems[n] = stack.enter_context(nc.semaphore(n))
                pool.append(n)
            self.dmaq[e] = {"pool": pool, "cum": {n: 0 for n in pool}, "nepoch": 0,
                            "open": None, "epochs": {}}

    def _close_epoch(self, q):
        st = self.dmaq[q]
        if st["open"] is not None:
            ep = st["epochs"][st["open"]]
            ep["closed"] = True
            st["cum"][ep["sem"]] += 16 * ep["n"]
            ep["target"] = st["cum"][ep["sem"]]
            st["open"] = None

    def _resolve(self, tok):
        if tok[0] == "E":
            _, q, eid = tok
            st = self.dmaq[q]
            ep = st["epochs"][eid]
            if not ep["closed"]:
                assert st["open"] == eid
                self._close_epoch(q)
            return ep["sem"], ep["target"]
        return tok

    def op(self, eng, fn, reads=(), writes=(), dma=False, tag=""):
        wl = {}

        def addw(tok):
            sn, sv = self._resolve(tok)
            if wl.get(sn, 0) < sv:
                wl[sn] = sv

        for r in reads:
            for tok in self.writes.get(r, {}).values():
                addw(tok)
        for w in writes:
            for tok in self.readers.get(w, {}).values():
                addw(tok)
            for tok in self.writes.get(w, {}).values():
                addw(tok)

        if dma:
            st = self.dmaq[eng]
            if st["open"] is None:
                sem = st["pool"][st["nepoch"] % NPOOL]
                eid = st["nepoch"]
                st["nepoch"] += 1
                st["open"] = eid
                st["epochs"][eid] = {"sem": sem, "n": 0, "closed": False, "target": None}
                if st["cum"][sem] > 0 and wl.get(sem, 0) < st["cum"][sem]:
                    wl[sem] = st["cum"][sem]
            ep = st["epochs"][st["open"]]
            ep["n"] += 1
            done_tok = ("E", eng, st["open"])
            incsem = ep["sem"]
        else:
            semname = f"{eng}_x"
            self.serial[semname] += 1
            done_tok = (semname, self.serial[semname])
            incsem = semname

        waits = list(wl.items())
        self.ops[eng].append((waits, fn, done_tok, incsem, tag))
        key = done_tok if done_tok[0] == "E" else done_tok[0]
        for w in writes:
            self.writes[w] = {key: done_tok}
            self.readers[w] = {}
        for r in reads:
            self.readers.setdefault(r, {})[key] = done_tok
        return done_tok

    def fence_from(self, srcs, dsts):
        tok = {}
        for b in srcs:
            for kk2, t in self.writes.get(b, {}).items():
                tok[kk2] = t
            for kk2, t in self.readers.get(b, {}).items():
                if kk2 in tok and kk2.endswith("_x") if isinstance(kk2, str) else False:
                    if t[1] > tok[kk2][1]:
                        tok[kk2] = t
                else:
                    tok[kk2] = t
        for d in dsts:
            self.writes[d] = dict(tok)
            self.readers[d] = {}

    def barrier_end(self, done):
        pass

    def finalize(self, block):
        for q in self.dmaq:
            self._close_epoch(q)
        needed = {f"{e}_x": set() for e in ENGINES}
        for e in ENGINES:
            for waits, _, _, _, _ in self.ops[e]:
                for sn, sv in waits:
                    if sn.endswith("_x"):
                        needed[sn].add(sv)
        value_of = {}
        for sn, ss in needed.items():
            value_of[sn] = {s: v for v, s in enumerate(sorted(ss), start=1)}

        def wval(sn, sv):
            if sn.endswith("_x"):
                m = value_of[sn]
                if sv in m:
                    return m[sv]
                cand = [s for s in m if s <= sv]
                return m[max(cand)] if cand else 0
            return sv

        fw = {}
        for q, st in self.dmaq.items():
            for sem, cum in st["cum"].items():
                if cum > 0:
                    fw[sem] = cum
        lastwait = {}

        def engine_body(e):
            def run(eng):
                for waits, fn, done_tok, incsem, tag in self.ops[e]:
                    for sn, sv in waits:
                        val = wval(sn, sv)
                        if val <= 0 or lastwait.get((e, sn), -1) >= val:
                            continue
                        lastwait[(e, sn)] = val
                        eng.wait_ge(self.sems[sn], val)
                    ins = fn(eng)
                    if incsem.endswith("_x"):
                        if done_tok[1] in needed[incsem]:
                            assert ins is not None, f"op needs inc but returned None: {tag}"
                            ins.then_inc(self.sems[incsem], 1)
                    else:
                        assert ins is not None, f"dma op returned None: {tag}"
                        ins.then_inc(self.sems[incsem], 16)
                if e == "vector":
                    for sn, val in sorted(fw.items()):
                        if lastwait.get((e, sn), -1) < val:
                            eng.wait_ge(self.sems[sn], val)
            return run

        for e in ENGINES:
            getattr(block, e)(engine_body(e))


class K:
    def __init__(self, nc, stack, sch):
        self.nc = nc
        self.stack = stack
        self.sch = sch
        self.sb = {}
        self.ps = {}
        self.dram = {}

    def sbuf(self, name, shape, dtype):
        t = self.stack.enter_context(self.nc.sbuf_tensor(name, list(shape), dtype))
        self.sb[name] = t
        return t

    def psum(self, name):
        t = self.stack.enter_context(self.nc.psum_tensor(name, [128, 512], F32))
        self.ps[name] = t
        return t

    def dram_t(self, name, shape, dtype, kind="Internal"):
        t = self.nc.dram_tensor(name, list(shape), dtype, kind=kind)
        self.dram[name] = t
        return t

    def dma(self, eng, out, in_, reads, writes, tag=""):
        return self.sch.op(eng, lambda e: e.dma_start(out=out, in_=in_), reads, writes, dma=True, tag=tag)

    def tt(self, out, a, b, op, reads, writes, eng="vector", tag=""):
        return self.sch.op(eng, lambda e: e.tensor_tensor(out=out, in0=a, in1=b, op=op), reads, writes, tag=tag)

    def ts(self, out, a, s1, s2, op0, op1, reads, writes, eng="vector", tag=""):
        if s2 is None:
            return self.sch.op(eng, lambda e: e.tensor_scalar(out=out, in0=a, scalar1=s1, scalar2=None, op0=op0), reads, writes, tag=tag)
        return self.sch.op(eng, lambda e: e.tensor_scalar(out=out, in0=a, scalar1=s1, scalar2=s2, op0=op0, op1=op1), reads, writes, tag=tag)

    def stt(self, out, a, s, b, op0, op1, reads, writes, eng="vector", tag=""):
        return self.sch.op(eng, lambda e: e.scalar_tensor_tensor(out=out, in0=a, scalar=s, in1=b, op0=op0, op1=op1), reads, writes, tag=tag)

    def copy(self, out, a, reads, writes, eng="vector", tag=""):
        return self.sch.op(eng, lambda e: e.tensor_copy(out=out, in_=a), reads, writes, tag=tag)

    def act(self, out, a, func, bias=0.0, scale=1.0, reads=(), writes=(), tag=""):
        return self.sch.op("scalar", lambda e: e.activation(out=out, in_=a, func=func, bias=bias, scale=scale), reads, writes, tag=tag)

    def recip(self, out, a, reads, writes, tag=""):
        return self.sch.op("vector", lambda e: e.reciprocal(out=out, in_=a), reads, writes, tag=tag)

    def reduce(self, out, a, op, reads, writes, tag=""):
        return self.sch.op("vector", lambda e: e.tensor_reduce(out=out, in_=a, axis=AXX, op=op), reads, writes, tag=tag)

    def scan(self, out, d0, d1, init, op0, op1, reads, writes, tag=""):
        return self.sch.op("vector", lambda e: e.tensor_tensor_scan(out=out, data0=d0, data1=d1, initial=init, op0=op0, op1=op1), reads, writes, tag=tag)

    def mmgroup(self, mms, reads, writes, tag=""):
        def f(e):
            ins = None
            n = len(mms)
            for i, m in enumerate(mms):
                if len(m) == 3:
                    o, l, r = m
                    st, sp = (i == 0), (i == n - 1)
                else:
                    o, l, r, st, sp = m
                ins = e.matmul(out=o, lhsT=l, rhs=r, start=st, stop=sp)
            return ins
        return self.sch.op("tensor", f, reads, writes, tag=tag)


CC = {}
_cc_vals = []


def _cc(name, val):
    CC[name] = len(_cc_vals)
    _cc_vals.append(np.float32(val))


_cc("zero", 0.0)
_cc("one", 1.0)
_cc("eps35", 2.0 ** -35)
_cc("tiny", 1e-38)
_cc("ln_eps", 1e-5)
_cc("eps6", 1e-6)
_cc("b8ln2", 8.0 * LN2)
_cc("half", 0.5)
NCC = len(_cc_vals)

DC = {n: i for i, n in enumerate(
    ["flag", "bcol", "mbcol", "bs5", "bs4", "thr_fc", "mthr_fc", "thr_pj", "mthr_pj"])}
NDC = 12


def host_consts():
    cc = np.zeros((128, NCC), np.float32)
    for i, v in enumerate(_cc_vals):
        cc[:, i] = v
    return cc


def build_program(phases, debug_out=()):
    nc = bass.Bass("TRN2", target_bir_lowering=False, debug=False)
    stack = contextlib.ExitStack()
    sch = Sched(nc, stack)
    k = K(nc, stack, sch)
    dbg = set(debug_out)

    def dr(name, shape, dtype):
        kind = "ExternalOutput" if name in dbg else "Internal"
        return k.dram_t(name, shape, dtype, kind=kind)

    x_in = k.dram_t("x", [NSEQ, NFC, 128, SEQ], F32, kind="ExternalInput")
    cc_in = k.dram_t("cc", [128, NCC], F32, kind="ExternalInput")
    onesrow = k.dram_t("ones_row", [1, 128], F32, kind="ExternalInput")
    one11 = k.dram_t("one_11", [1, 1], F32, kind="ExternalInput")
    masks_in = k.dram_t("masks", [4, 128, 128], F32, kind="ExternalInput")
    bias_in = k.dram_t("biases", [128, 160], F32, kind="ExternalInput")
    dmask_in = k.dram_t("dmask", [NSEQ, NFC, 128, SEQ], BF16, kind="ExternalInput")
    w_l0qkv = k.dram_t("w_l0qkv", [3, D, D], F32, kind="ExternalInput")
    w_l0o = k.dram_t("w_l0o", [D, D], F32, kind="ExternalInput")
    w_fcT = k.dram_t("w_fcT", [D, HFF], F32, kind="ExternalInput")
    w_pjT = k.dram_t("w_pjT", [HFF, D], F32, kind="ExternalInput")
    w_l1p = k.dram_t("w_l1p", [5, D, C1], F32, kind="ExternalInput")
    w_l1o = k.dram_t("w_l1o", [C1, D], F32, kind="ExternalInput")
    w_1T = k.dram_t("w_1T", [D, HFF], F32, kind="ExternalInput")
    w_2T = k.dram_t("w_2T", [HFF, D], F32, kind="ExternalInput")
    xout = k.dram_t("xout", [NSEQ, NFC, 128, SEQ], F32, kind="ExternalOutput")

    x1_d = dr("x1", [NSEQ, NFC, 128, SEQ], F32)
    x2_d = dr("x2", [NSEQ, NFC, 128, SEQ], F32)
    xsp_d = dr("xspl", [NSEQ, 2, NFC, 128, SEQ], BF16)
    wsp = dr("wsp", [2, 4, D, D], BF16)
    wfcq_d = dr("wfcq", [D, HFF], BF16)
    wpjq_d = dr("wpjq", [HFF, D], BF16)
    w1q_d = dr("w1q", [D, HFF], FP8)
    w2q_d = dr("w2q", [HFF, D], FP8)
    wl1sp = dr("wl1sp", [2, 5, D, C1], BF16)
    wl1osp = dr("wl1osp", [2, C1, D], BF16)
    qfm_d = dr("qfm", [NSEQ, C1, SEQ], F32)
    kfm_d = dr("kfm", [NSEQ, C1, SEQ], F32)
    gfm_d = dr("gfm", [NSEQ, C1, SEQ], F32)
    ktm_d = dr("ktm", [NSEQ, 32, 128, C1], F32)
    vtm_d = dr("vtm", [NSEQ, 32, 128, C1], F32)
    rows_d = dr("rows", [NSEQ, 4, SEQ], F32)
    lac_d = dr("lacols", [NSEQ, 32, 128, 4], F32)
    x3_d = dr("x3", [NSEQ, NFC, 128, SEQ], F32)
    ofm_d = dr("ofm", [NSEQ, C1, SEQ], F32)

    ccsb = k.sbuf("ccsb", [128, NCC], F32)
    dcsb = k.sbuf("dcsb", [128, NDC], F32)
    accsb = k.sbuf("accsb", [128, 64], F32)
    ones_sb = k.sbuf("ones1", [1, 128], F32)
    one11_sb = k.sbuf("one11", [1, 1], F32)
    mask_sb = k.sbuf("masksb", [128, 4, 128], F32)
    bias_sb = k.sbuf("biassb", [128, 160], F32)
    WB = k.sbuf("WB", [128, 32 * 1024], BF16)
    WS = k.sbuf("WS", [128, 4, 1024], BF16)
    H = k.sbuf("H", [128, NHC, 512], F32)
    A1 = k.sbuf("A1", [128, NFC, 512], F32)
    A2 = k.sbuf("A2", [128, NFC, 512], F32)
    BH1 = k.sbuf("BH1", [128, NFC, 512], BF16)
    BH2 = k.sbuf("BH2", [128, NFC, 512], BF16)
    BQ8 = k.sbuf("BQ8", [128, 2, NFC, 512], FP8)
    Q8S = BH1[:].bitcast(FP8)
    R1 = k.sbuf("R1", [128, 512], F32)
    R2 = k.sbuf("R2", [128, 512], F32)
    ST = k.sbuf("STr", [1, 1024], F32)
    PROW = k.sbuf("PROW", [2, 2 * 512], F32)
    pb = [k.psum(f"pb{i}") for i in range(8)]
    Hf = H[:].rearrange("p a b -> p (a b)")

    def ccol(name):
        return ccsb[:, CC[name]:CC[name] + 1]

    def dcol(name):
        return dcsb[:, DC[name]:DC[name] + 1]

    def flat(t, n):
        return t[:].rearrange("p a b -> p (a b)")[:, 0:n]

    k.dma("sync", ccsb[:], cc_in.ap(), [], ["ccsb"], tag="ld_cc")
    k.dma("sync", ones_sb[:], onesrow.ap(), [], ["ones1"], tag="ld_ones")
    k.dma("sync", one11_sb[:], one11.ap(), [], ["one11"], tag="ld_one11")
    k.dma("sync", mask_sb[:], masks_in.ap().transpose([1, 0, 2]), [], ["masksb"], tag="ld_masks")
    k.dma("sync", bias_sb[:], bias_in.ap(), [], ["biassb"], tag="ld_bias")

    k.ts(dcol("flag"), ccol("eps35"), 1.0, 1.0, ALU.add, ALU.subtract, ["ccsb"], ["dcsb"], tag="flag0")
    k.ts(dcol("flag"), dcol("flag"), 0.0, None, ALU.is_gt, None, ["dcsb"], ["dcsb"], tag="flag1")
    k.ts(dcol("bcol"), dcol("flag"), B64, B32, ALU.mult, ALU.max, ["dcsb"], ["dcsb"], tag="bcol")
    k.ts(dcol("mbcol"), dcol("flag"), -B64, -B32, ALU.mult, ALU.min, ["dcsb"], ["dcsb"], tag="mbcol")
    for nm, M in (("bs5", 10), ("bs4", 3)):
        base = float(math.log(1.5) + 23 * LN2 - M * LN2)
        k.ts(dcol(nm), dcol("flag"), 29.0 * LN2, base, ALU.mult, ALU.add, ["dcsb"], ["dcsb"], tag=f"dc_{nm}")

    def wprep_split(src_ap, n_ic, n_out, hi_ap, lo_ap, name):
        for icx in range(n_ic):
            wt = flat(H, n_out)
            hi = flat(BH1, n_out)
            lo = flat(BH2, n_out)
            k.dma("sync", wt, src_ap[icx * 128:(icx + 1) * 128, :], ["H"], ["H"], tag=f"{name}_ld{icx}")
            k.copy(hi, wt, ["H"], ["BH1"], tag=f"{name}_hi{icx}")
            k.tt(lo, wt, hi, ALU.subtract, ["H", "BH1"], ["BH2"], tag=f"{name}_lo{icx}")
            k.dma("gpsimd", hi_ap[icx * 128:(icx + 1) * 128, :], hi, ["BH1"], [f"d_{name}"], tag=f"{name}_sthi{icx}")
            k.dma("gpsimd", lo_ap[icx * 128:(icx + 1) * 128, :], lo, ["BH2"], [f"d_{name}"], tag=f"{name}_stlo{icx}")

    if "p0w" in phases:
        for wi in range(3):
            wprep_split(w_l0qkv.ap()[wi], NFC, D, wsp.ap()[0, wi], wsp.ap()[1, wi], f"l0w{wi}")
        wprep_split(w_l0o.ap(), NFC, D, wsp.ap()[0, 3], wsp.ap()[1, 3], "l0wo")
        for wi in range(5):
            wprep_split(w_l1p.ap()[wi], NFC, C1, wl1sp.ap()[0, wi], wl1sp.ap()[1, wi], f"l1w{wi}")
        wprep_split(w_l1o.ap(), 1, D, wl1osp.ap()[0], wl1osp.ap()[1], "l1wo")

    def emit_ternary(src_ap, n_ic, n_out, thr_name, dst_ap, name):
        for icx in range(n_ic):
            wt = flat(H, n_out)
            ab = flat(A2, n_out)
            k.dma("sync", wt, src_ap[icx * 128:(icx + 1) * 128, :], ["H"], ["H"], tag=f"{name}_gld{icx}")
            k.act(ab, wt, ACT.Abs, reads=["H"], writes=["A2"], tag=f"{name}_abs{icx}")
            k.reduce(accsb[:, 32 + icx:33 + icx], ab, ALU.add, ["A2"], ["accsb"], tag=f"{name}_red{icx}")
        k.reduce(accsb[:, 32:33], accsb[:, 32:32 + n_ic], ALU.add,
                 ["accsb"], ["accsb"], tag=f"{name}_colred")
        k.mmgroup([(pb[0][0:1, 0:1], accsb[:, 32:33], ccol("one"))], ["accsb", "ccsb"], ["pb0"], tag=f"{name}_tot")
        tot = ST[0:1, 0:1]
        k.ts(tot, pb[0][0:1, 0:1], 1.0 / (n_ic * 128 * n_out), None, ALU.mult, None, ["pb0"], ["STr"], tag=f"{name}_tot1")
        k.ts(tot, tot, 1e-6, 0.5, ALU.add, ALU.mult, ["STr"], ["STr"], tag=f"{name}_thrv")
        k.mmgroup([(pb[0][:, 1:2], ones_sb[:], tot)], ["STr", "ones1"], ["pb0"], tag=f"{name}_bc")
        k.copy(dcol(thr_name), pb[0][:, 1:2], ["pb0"], ["dcsb"], tag=f"{name}_thr")
        k.ts(dcol("m" + thr_name), pb[0][:, 1:2], -1.0, None, ALU.mult, None, ["pb0"], ["dcsb"], tag=f"{name}_mthr")
        for icx in range(n_ic):
            wt = flat(H, n_out)
            p_ = flat(A2, n_out)
            n_ = Hf[:, 8192:8192 + n_out]
            qv = flat(BH1, n_out)
            k.dma("sync", wt, src_ap[icx * 128:(icx + 1) * 128, :], ["H"], ["H"], tag=f"{name}_qld{icx}")
            k.ts(p_, wt, dcol(thr_name), None, ALU.is_ge, None, ["H", "dcsb"], ["A2"], tag=f"{name}_p{icx}")
            k.ts(n_, wt, dcol("m" + thr_name), None, ALU.is_le, None, ["H", "dcsb"], ["H"], tag=f"{name}_n{icx}")
            k.tt(qv, p_, n_, ALU.subtract, ["A2", "H"], ["BH1"], tag=f"{name}_q{icx}")
            k.dma("gpsimd", dst_ap[icx * 128:(icx + 1) * 128, :], qv, ["BH1"], [f"d_{name}"], tag=f"{name}_qst{icx}")

    if "p0t" in phases:
        emit_ternary(w_fcT.ap(), NFC, HFF, "thr_fc", wfcq_d.ap(), "tfc")
        emit_ternary(w_pjT.ap(), NHC, D, "thr_pj", wpjq_d.ap(), "tpj")

    def emit_q4exact(src, n, dst, reads, writes, name, relu=False, t1b=None, t2b=None,
                     r1="A2", r2="H"):
        t1 = t1b if t1b is not None else flat(A2, n)
        t2 = t2b if t2b is not None else Hf[:, 8192:8192 + n]
        r_ = list(reads)
        k.act(t1, src, ACT.Abs, reads=r_, writes=[r1], tag=f"{name}_abs")
        k.act(t1, t1, ACT.Ln, bias=ccol("tiny"), reads=[r1, "ccsb"], writes=[r1], tag=f"{name}_ln")
        k.ts(t1, t1, 1.0 / LN2, -0.5, ALU.mult, ALU.add, [r1], [r1], tag=f"{name}_v")
        k.ts(t1, t1, dcol("bcol"), dcol("mbcol"), ALU.add, ALU.add, [r1, "dcsb"], [r1], tag=f"{name}_e0")
        k.ts(t1, t1, -6.0, 8.0, ALU.max, ALU.min, [r1], [r1], tag=f"{name}_e4")
        k.act(t1, t1, ACT.Exp, bias=ccol("b8ln2"), scale=-LN2, reads=[r1, "ccsb"], writes=[r1], tag=f"{name}_p")
        k.ts(t1, t1, dcol("bcol"), dcol("mbcol"), ALU.add, ALU.add, [r1, "dcsb"], [r1], tag=f"{name}_pe")
        k.stt(t2, src, 2.0 ** -5, t1, ALU.mult, ALU.mult, r_ + [r1], [r2], tag=f"{name}_t")
        k.ts(t2, t2, dcol("bcol"), dcol("mbcol"), ALU.add, ALU.add, [r2, "dcsb"], [r2], tag=f"{name}_n")
        k.act(t1, t1, ACT.Ln, reads=[r1], writes=[r1], tag=f"{name}_rl")
        k.ts(t1, t1, -1.0, 14.0 * LN2, ALU.mult, ALU.add, [r1], [r1], tag=f"{name}_rn")
        k.act(t1, t1, ACT.Exp, reads=[r1], writes=[r1], tag=f"{name}_rx")
        k.ts(t1, t1, dcol("bcol"), dcol("mbcol"), ALU.add, ALU.add, [r1, "dcsb"], [r1], tag=f"{name}_se")
        k.stt(dst, t2, 2.0 ** -9, t1, ALU.mult, ALU.mult, [r1, r2], list(writes), tag=f"{name}_q")

    def emit_wq4(src_ap, n_ic, n_out, dst_ap, name):
        for icx in range(n_ic):
            wt = flat(H, n_out)
            qv = flat(BH1, n_out)
            k.dma("sync", wt, src_ap[icx * 128:(icx + 1) * 128, :], ["H"], ["H"], tag=f"{name}_ld{icx}")
            emit_q4exact(wt, n_out, qv, ["H", "dcsb", "ccsb"], ["BH1"], f"{name}{icx}")
            if name == "qw1" and icx == 0 and "q4dbg" in dbg:
                q4dump = k.dram_t("q4dump", [3, 128, 512], F32, kind="ExternalOutput")
                k.dma("gpsimd", q4dump.ap()[0], flat(A2, 512), ["A2"], [], tag="q4d_t1")
                k.dma("gpsimd", q4dump.ap()[1], Hf[:, 8192:8192 + 512], ["H"], [], tag="q4d_t2")
                dn = k.dma("gpsimd", q4dump.ap()[2], wt[:, 0:512], ["H"], [], tag="q4d_wt")
                sch.barrier_end(dn)
            k.dma("gpsimd", dst_ap[icx * 128:(icx + 1) * 128, :], qv, ["BH1"], [f"d_{name}"], tag=f"{name}_st{icx}")

    if "p0q" in phases:
        emit_wq4(w_1T.ap(), NFC, HFF, w1q_d.ap(), "qw1")
        emit_wq4(w_2T.ap(), NHC, D, w2q_d.ap(), "qw2")
        emit_q4exact(bias_sb[:, 32:64], 32, bias_sb[:, 32:64], ["biassb", "dcsb", "ccsb"], ["biassb"], "qb1")
        emit_q4exact(bias_sb[:, 64:72], 8, bias_sb[:, 64:72], ["biassb", "dcsb", "ccsb"], ["biassb"], "qb2")

    def emit_ln(xap, n_fc, T, deps, name, sq, sqres, hi=None, lo=None, hires=None, lores=None,
               xn=None, xnres=None, psA=0, psB=1, par=0):
        st_t = ST if par == 0 else PROW[0:1, :]
        str_ = "STr" if par == 0 else "PROW0"
        mu2 = (R1 if par == 0 else R2)[0:1, 0:T]
        m2t = "R1" if par == 0 else "R2"
        k.act(sq, xap, ACT.Square, reads=deps, writes=[sqres], tag=f"{name}_sq")
        k.mmgroup([(pb[psA][0:1, 0:T], ccol("one"), xap[:, fc, :]) for fc in range(n_fc)],
                  deps + ["ccsb"], [f"pb{psA}"], tag=f"{name}_must")
        k.mmgroup([(pb[psB][0:1, 0:T], ccol("one"), sq[:, fc, :]) for fc in range(n_fc)],
                  [sqres, "ccsb"], [f"pb{psB}"], tag=f"{name}_sqst")
        mu = st_t[0:1, 0:T]
        sqr = st_t[0:1, 512:512 + T]
        var = sqr
        rr = sqr
        mur = mu
        nfeat = n_fc * 128
        k.ts(mu, pb[psA][0:1, 0:T], 1.0 / nfeat, None, ALU.mult, None, [f"pb{psA}"], [str_], tag=f"{name}_mu")
        k.ts(sqr, pb[psB][0:1, 0:T], 1.0 / nfeat, None, ALU.mult, None, [f"pb{psB}", str_], [str_], tag=f"{name}_sqm")
        k.tt(mu2, mu, mu, ALU.mult, [str_], [m2t], tag=f"{name}_mu2")
        k.tt(var, sqr, mu2, ALU.subtract, [str_, m2t], [str_], tag=f"{name}_var")
        k.act(var, var, ACT.Ln, bias=ccsb[0:1, CC["ln_eps"]:CC["ln_eps"] + 1],
              reads=[str_, "ccsb"], writes=[str_], tag=f"{name}_lnv")
        k.ts(var, var, -0.5, None, ALU.mult, None, [str_], [str_], tag=f"{name}_hf")
        k.act(rr, var, ACT.Exp, reads=[str_], writes=[str_], tag=f"{name}_r")
        k.tt(mur, mu, rr, ALU.mult, [str_], [str_], tag=f"{name}_mur")
        k.mmgroup([(pb[psA][:, 0:T], ones_sb[:], rr)], [str_, "ones1"], [f"pb{psA}"], tag=f"{name}_rbc")
        k.mmgroup([(pb[psB][:, 0:T], ones_sb[:], mur)], [str_, "ones1"], [f"pb{psB}"], tag=f"{name}_murbc")
        rb = pb[psA][:, 0:T].unsqueeze(1).broadcast_to([128, n_fc, T])
        murb = pb[psB][:, 0:T].unsqueeze(1).broadcast_to([128, n_fc, T])
        k.tt(sq, xap, rb, ALU.mult, deps + [f"pb{psA}"], [sqres], tag=f"{name}_t4")
        if xn is not None:
            k.tt(xn, sq, murb, ALU.subtract, [sqres, f"pb{psB}"], [xnres], tag=f"{name}_xn")
            if hi is not None:
                k.copy(hi, xn, [xnres], [hires], tag=f"{name}_hi")
                k.tt(lo, xn, hi, ALU.subtract, [xnres, hires], [lores], tag=f"{name}_lo")
        else:
            k.tt(hi, sq, murb, ALU.subtract, [sqres, f"pb{psB}"], [hires], tag=f"{name}_hi")
            k.tt(sq, sq, murb, ALU.subtract, [sqres, f"pb{psB}"], [sqres], tag=f"{name}_xnf")
            k.tt(lo, sq, hi, ALU.subtract, [sqres, hires], [lores], tag=f"{name}_lo")

    def wb(off_kb, n):
        start = off_kb * 512
        return WB[:, start:start + n]

    def wmat(idx):
        return wb(16 * idx, NFC * D).rearrange("p (a b) -> p a b", a=NFC)

    def load_wmat(idx, src_ap, tag, dname):
        return k.dma("sync", wmat(idx), src_ap.rearrange("(a p) b -> p a b", p=128),
                     [f"WB{idx}", dname], [f"WB{idx}"], tag=tag)

    T1 = 512
    NT1 = SEQ // T1
    KVACC = accsb[:, 0:2 * NFC]
    KSACC = accsb[:, 2 * NFC:4 * NFC]
    sch.fence_from(["H"], ["Hq0", "Hq1", "Hq2", "Hq3"])

    if "p1" in phases:
        load_wmat(0, wsp.ap()[0, 1], "ld_wkh", "d_l0w1")
        load_wmat(1, wsp.ap()[1, 1], "ld_wkl", "d_l0w1")
        load_wmat(2, wsp.ap()[0, 2], "ld_wvh", "d_l0w2")
        load_wmat(3, wsp.ap()[1, 2], "ld_wvl", "d_l0w2")
        k.ts(accsb[:, 0:4 * NFC], accsb[:, 0:4 * NFC], 0.0, None, ALU.mult, None, ["accsb"], ["accsb"], tag="p1_zero")
        for s in range(NSEQ):
            for it in range(NT1):
                t0 = it * T1
                xt = A1[:, :, 0:T1]
                k.dma("sync", xt, x_in.ap()[s, :, :, t0:t0 + T1].transpose([1, 0, 2]), ["A1"], ["A1"], tag=f"p1_ldx{s}_{it}")
                emit_ln(xt, NFC, T1, ["A1"], f"p1ln{s}_{it}", A2[:, :, 0:T1], "A2",
                        hi=BH1[:, :, 0:T1], lo=BH2[:, :, 0:T1], hires="BH1", lores="BH2")
                k.dma("scalar", xsp_d.ap()[s, 0, :, :, t0:t0 + T1].transpose([1, 0, 2]), BH1[:, :, 0:T1],
                      ["BH1"], [f"xsp0:{s}:{it}"], tag=f"p1_sh{s}_{it}")
                k.dma("scalar", xsp_d.ap()[s, 1, :, :, t0:t0 + T1].transpose([1, 0, 2]), BH2[:, :, 0:T1],
                      ["BH2"], [f"xsp1:{s}:{it}"], tag=f"p1_sl{s}_{it}")
                if s == 0 and it == 0 and "p1dbg" in dbg:
                    p1dump = k.dram_t("p1dump", [4, 128, 512], F32, kind="ExternalOutput")
                    p1dumpb = k.dram_t("p1dumpb", [2, 128, NFC, 512], BF16, kind="ExternalOutput")
                    k.dma("gpsimd", p1dump.ap()[0], A1[:, 0, 0:T1], ["A1"], [], tag="p1d_x")
                    k.dma("gpsimd", p1dump.ap()[1], R1[:, 0:T1], ["R1"], [], tag="p1d_r")
                    k.dma("gpsimd", p1dump.ap()[2], R2[:, 0:T1], ["R2"], [], tag="p1d_mur")
                    k.dma("gpsimd", p1dumpb.ap()[0], BH1[:, :, 0:T1], ["BH1"], [], tag="p1d_hi")
                    dn = k.dma("gpsimd", p1dumpb.ap()[1], BH2[:, :, 0:T1], ["BH2"], [], tag="p1d_lo")
                    sch.barrier_end(dn)
                for pi, (whi, wlo, bcol0, wres) in enumerate([(0, 1, 8, "WB0"), (2, 3, 16, "WB2")]):
                    isk = pi == 0
                    for oc in range(NFC):
                        bank = 2 + (pi * NFC + oc) % 4
                        mms = []
                        for kc in range(NFC):
                            lh = wmat(whi)[:, kc, oc * 128:(oc + 1) * 128]
                            ll = wmat(whi + 1)[:, kc, oc * 128:(oc + 1) * 128]
                            mms.append((pb[bank][:, 0:T1], lh, BH1[:, kc, 0:T1]))
                            mms.append((pb[bank][:, 0:T1], lh, BH2[:, kc, 0:T1]))
                            mms.append((pb[bank][:, 0:T1], ll, BH1[:, kc, 0:T1]))
                        k.mmgroup(mms, ["BH1", "BH2", f"WB{whi}", f"WB{whi + 1}"], [f"pb{bank}"],
                                  tag=f"p1_mm{s}_{it}_{pi}_{oc}")
                        dstb = H[:, (8 + oc) if isk else oc, 0:T1]
                        k.act(dstb, pb[bank][:, 0:T1], ACT.Identity, bias=bias_sb[:, bcol0 + oc:bcol0 + oc + 1],
                              reads=[f"pb{bank}", "biassb"], writes=["Hq1" if isk else "Hq0"],
                              tag=f"p1_ev{s}_{it}_{pi}_{oc}")
                kk_ = H[:, 8:16, 0:T1]
                mm_ = A2[:, :, 0:T1]
                k.ts(mm_, kk_, 0.0, None, ALU.min, None, ["Hq1"], ["A2"], tag=f"p1_min{s}_{it}")
                k.act(mm_, mm_, ACT.Exp, reads=["A2"], writes=["A2"], tag=f"p1_exp{s}_{it}")
                k.stt(kk_, kk_, 1.0, mm_, ALU.add, ALU.max, ["Hq1", "A2"], ["Hq1"], tag=f"p1_fm{s}_{it}")
                kv = A2[:, :, 0:T1]
                k.tt(kv, kk_, H[:, 0:NFC, 0:T1], ALU.mult, ["Hq0", "Hq1"], ["A2"], tag=f"p1_kv{s}_{it}")
                pc1 = accsb[:, 40:40 + NFC].unsqueeze(2)
                pc2 = accsb[:, 48:48 + NFC].unsqueeze(2)
                k.reduce(pc1, kv, ALU.add, ["A2"], ["accsb"], tag=f"p1_kvr{s}_{it}")
                k.reduce(pc2, kk_, ALU.add, ["Hq1"], ["accsb"], tag=f"p1_ksr{s}_{it}")
                k.tt(KVACC[:, s * NFC:(s + 1) * NFC], KVACC[:, s * NFC:(s + 1) * NFC],
                     accsb[:, 40:40 + NFC], ALU.add, ["accsb"], ["accsb"], tag=f"p1_kva{s}_{it}")
                k.tt(KSACC[:, s * NFC:(s + 1) * NFC], KSACC[:, s * NFC:(s + 1) * NFC],
                     accsb[:, 48:48 + NFC], ALU.add, ["accsb"], ["accsb"], tag=f"p1_ksa{s}_{it}")

    if "p2a" in phases:
        load_wmat(0, wsp.ap()[0, 0], "ld_wqh", "d_l0w0")
        load_wmat(1, wsp.ap()[1, 0], "ld_wql", "d_l0w0")
        load_wmat(2, wsp.ap()[0, 3], "ld_woh", "d_l0wo")
        load_wmat(3, wsp.ap()[1, 3], "ld_wol", "d_l0wo")
        for s in range(NSEQ):
            kvb = KVACC[:, s * NFC:(s + 1) * NFC].unsqueeze(2).broadcast_to([128, NFC, T1])
            ksb = KSACC[:, s * NFC:(s + 1) * NFC].unsqueeze(2).broadcast_to([128, NFC, T1])
            for it in range(NT1):
                t0 = it * T1
                xdn = f"x1:{s}:{it}"
                k.dma("scalar", x1_d.ap()[s, :, :, t0:t0 + T1], x_in.ap()[s, :, :, t0:t0 + T1],
                      [xdn], [xdn], tag=f"p2a_pc{s}_{it}")
                k.dma("sync", BH1[:, :, 0:T1], xsp_d.ap()[s, 0, :, :, t0:t0 + T1].transpose([1, 0, 2]),
                      ["BH1", f"xsp0:{s}:{it}"], ["BH1"], tag=f"p2a_lh{s}_{it}")
                k.dma("sync", BH2[:, :, 0:T1], xsp_d.ap()[s, 1, :, :, t0:t0 + T1].transpose([1, 0, 2]),
                      ["BH2", f"xsp1:{s}:{it}"], ["BH2"], tag=f"p2a_ll{s}_{it}")
                for oc in range(NFC):
                    bank = 2 + oc % 4
                    mms = []
                    for kc in range(NFC):
                        lh = wmat(0)[:, kc, oc * 128:(oc + 1) * 128]
                        ll = wmat(1)[:, kc, oc * 128:(oc + 1) * 128]
                        mms.append((pb[bank][:, 0:T1], lh, BH1[:, kc, 0:T1]))
                        mms.append((pb[bank][:, 0:T1], lh, BH2[:, kc, 0:T1]))
                        mms.append((pb[bank][:, 0:T1], ll, BH1[:, kc, 0:T1]))
                    k.mmgroup(mms, ["BH1", "BH2", "WB0", "WB1"], [f"pb{bank}"], tag=f"p2a_qmm{s}_{it}_{oc}")
                    k.act(H[:, oc, 0:T1], pb[bank][:, 0:T1], ACT.Identity, bias=bias_sb[:, oc:oc + 1],
                          reads=[f"pb{bank}", "biassb"], writes=["Hq0"], tag=f"p2a_qev{s}_{it}_{oc}")
                qq = H[:, 0:NFC, 0:T1]
                mm_ = A2[:, :, 0:T1]
                k.ts(mm_, qq, 0.0, None, ALU.min, None, ["Hq0"], ["A2"], tag=f"p2a_min{s}_{it}")
                k.act(mm_, mm_, ACT.Exp, reads=["A2"], writes=["A2"], tag=f"p2a_exp{s}_{it}")
                k.stt(qq, qq, 1.0, mm_, ALU.add, ALU.max, ["Hq0", "A2"], ["Hq0"], tag=f"p2a_fm{s}_{it}")
                u = A2[:, :, 0:T1]
                k.tt(u, qq, ksb, ALU.mult, ["Hq0", "accsb"], ["A2"], tag=f"p2a_u{s}_{it}")
                k.act(u, u, ACT.Ln, bias=ccol("eps6"), reads=["A2", "ccsb"], writes=["A2"], tag=f"p2a_ul{s}_{it}")
                k.ts(u, u, -1.0, None, ALU.mult, None, ["A2"], ["A2"], tag=f"p2a_un{s}_{it}")
                k.act(u, u, ACT.Exp, reads=["A2"], writes=["A2"], tag=f"p2a_rc{s}_{it}")
                at = H[:, 8:16, 0:T1]
                k.tt(at, qq, u, ALU.mult, ["Hq0", "A2"], ["Hq1"], tag=f"p2a_a1{s}_{it}")
                k.tt(at, at, kvb, ALU.mult, ["Hq1", "accsb"], ["Hq1"], tag=f"p2a_a2{s}_{it}")
                k.copy(BH1[:, :, 0:T1], at, ["Hq1"], ["BH1"], tag=f"p2a_hi{s}_{it}")
                k.tt(BH2[:, :, 0:T1], at, BH1[:, :, 0:T1], ALU.subtract, ["Hq1", "BH1"], ["BH2"], tag=f"p2a_lo{s}_{it}")
                for oc in range(NFC):
                    bank = 2 + oc % 4
                    mms = []
                    for kc in range(NFC):
                        lh = wmat(2)[:, kc, oc * 128:(oc + 1) * 128]
                        ll = wmat(3)[:, kc, oc * 128:(oc + 1) * 128]
                        mms.append((pb[bank][:, 0:T1], lh, BH1[:, kc, 0:T1]))
                        mms.append((pb[bank][:, 0:T1], lh, BH2[:, kc, 0:T1]))
                        mms.append((pb[bank][:, 0:T1], ll, BH1[:, kc, 0:T1]))
                    k.mmgroup(mms, ["BH1", "BH2", "WB2", "WB3"], [f"pb{bank}"], tag=f"p2a_omm{s}_{it}_{oc}")
                    k.act(H[:, 16 + oc, 0:T1], pb[bank][:, 0:T1], ACT.Gelu_apprx_tanh,
                          bias=bias_sb[:, 24 + oc:24 + oc + 1],
                          reads=[f"pb{bank}", "biassb"], writes=["Hq2"], tag=f"p2a_oev{s}_{it}_{oc}")
                emit_ln(H[:, 16:24, 0:T1], NFC, T1, ["Hq2"], f"p2aln2{s}_{it}", A2[:, :, 0:T1], "A2",
                        xn=H[:, 24:32, 0:T1], xnres="Hq3")
                k.sch.op("gpsimd", lambda e, s=s, t0=t0: e.dma_start(
                    out=x1_d.ap()[s, :, :, t0:t0 + T1].transpose([1, 0, 2]),
                    in_=H[:, 24:32, 0:T1], accum_op=ALU.add),
                    ["Hq3", xdn], [xdn], dma=True, tag=f"p2a_st{s}_{it}")

    if "p2b" in phases:
        WBv = WB[:, 0:NFC * HFF].rearrange("p (a b) -> p a b", a=NFC)
        k.dma("sync", WBv, wfcq_d.ap().rearrange("(a p) b -> p a b", p=128),
              ["WB0", "WB1", "WB2", "WB3", "d_tfc"], ["WB0", "WB1", "WB2", "WB3"], tag="p2b_ldwfc")
        WSv = WS[:, :, 0:1024]
        for s in range(NSEQ):
            for it in range(NT1):
                t0 = it * T1
                xdn = f"x2:{s}:{it}"
                x1n = f"x1:{s}:{it}"
                xt = A1[:, :, 0:T1]
                k.dma("sync", xt, x1_d.ap()[s, :, :, t0:t0 + T1].transpose([1, 0, 2]), ["A1", x1n], ["A1"], tag=f"p2b_ldx{s}_{it}")
                k.dma("scalar", x2_d.ap()[s, :, :, t0:t0 + T1], x1_d.ap()[s, :, :, t0:t0 + T1],
                      [xdn, x1n], [xdn], tag=f"p2b_pc{s}_{it}")
                emit_ln(xt, NFC, T1, ["A1"], f"p2bln{s}_{it}", A2[:, :, 0:T1], "A2",
                        hi=BH1[:, :, 0:T1], lo=BH2[:, :, 0:T1], hires="BH1", lores="BH2")
                for hc in range(NHC):
                    bank = 2 + hc % 2
                    mms = []
                    for kc in range(NFC):
                        wq_ = WBv[:, kc, hc * 128:(hc + 1) * 128]
                        mms.append((pb[bank][:, 0:T1], wq_, BH1[:, kc, 0:T1]))
                        mms.append((pb[bank][:, 0:T1], wq_, BH2[:, kc, 0:T1]))
                    k.mmgroup(mms, ["BH1", "BH2", "WB0"], [f"pb{bank}"], tag=f"p2b_hmm{s}_{it}_{hc}")
                    k.act(H[:, hc, 0:T1], pb[bank][:, 0:T1], ACT.Gelu,
                          reads=[f"pb{bank}"], writes=[f"Hq{hc // 8}"], tag=f"p2b_hev{s}_{it}_{hc}")
                HQALL = ["Hq0", "Hq1", "Hq2", "Hq3"]
                k.mmgroup([(pb[0][0:1, 0:T1], ccol("one"), H[:, hc, 0:T1]) for hc in range(NHC)],
                          HQALL + ["ccsb"], ["pb0"], tag=f"p2b_must{s}_{it}")
                for g in range(4):
                    k.act(A2[:, :, 0:T1], H[:, g * NFC:(g + 1) * NFC, 0:T1], ACT.Square,
                          reads=[f"Hq{g}"], writes=["A2"], tag=f"p2b_sq{s}_{it}_{g}")
                    k.mmgroup([(pb[1][0:1, 0:T1], ccol("one"), A2[:, fc, 0:T1],
                                g == 0 and fc == 0, g == 3 and fc == NFC - 1) for fc in range(NFC)],
                              ["A2", "ccsb"], ["pb1"], tag=f"p2b_sqst{s}_{it}_{g}")
                mu = ST[0:1, 0:T1]
                sqr = ST[0:1, 512:512 + T1]
                rr = sqr
                mur = mu
                k.ts(mu, pb[0][0:1, 0:T1], 1.0 / HFF, None, ALU.mult, None, ["pb0"], ["STr"], tag=f"p2b_mu{s}_{it}")
                k.ts(sqr, pb[1][0:1, 0:T1], 1.0 / HFF, None, ALU.mult, None, ["pb1", "STr"], ["STr"], tag=f"p2b_sqm{s}_{it}")
                mu2b = R1[0:1, 0:T1]
                k.tt(mu2b, mu, mu, ALU.mult, ["STr"], ["R1"], tag=f"p2b_mu2{s}_{it}")
                k.tt(sqr, sqr, mu2b, ALU.subtract, ["STr", "R1"], ["STr"], tag=f"p2b_var{s}_{it}")
                k.act(sqr, sqr, ACT.Ln, bias=ccsb[0:1, CC["ln_eps"]:CC["ln_eps"] + 1],
                      reads=["STr", "ccsb"], writes=["STr"], tag=f"p2b_lnv{s}_{it}")
                k.ts(sqr, sqr, -0.5, None, ALU.mult, None, ["STr"], ["STr"], tag=f"p2b_hf{s}_{it}")
                k.act(rr, sqr, ACT.Exp, reads=["STr"], writes=["STr"], tag=f"p2b_r{s}_{it}")
                k.tt(mur, mu, rr, ALU.mult, ["STr"], ["STr"], tag=f"p2b_mur{s}_{it}")
                k.mmgroup([(pb[0][:, 0:T1], ones_sb[:], rr)], ["STr", "ones1"], ["pb0"], tag=f"p2b_rbc{s}_{it}")
                k.mmgroup([(pb[1][:, 0:T1], ones_sb[:], mur)], ["STr", "ones1"], ["pb1"], tag=f"p2b_murbc{s}_{it}")
                k.copy(R1[:, 0:T1], pb[0][:, 0:T1], ["pb0"], ["R1"], tag=f"p2b_rbev{s}_{it}")
                k.copy(R2[:, 0:T1], pb[1][:, 0:T1], ["pb1"], ["R2"], tag=f"p2b_murbev{s}_{it}")
                rb4 = R1[:, 0:T1].unsqueeze(1).broadcast_to([128, 4, T1])
                murb4 = R2[:, 0:T1].unsqueeze(1).broadcast_to([128, 4, T1])
                for gg in range(8):
                    par = gg % 2
                    csl = slice(gg * 4, (gg + 1) * 4)
                    bsl = slice(par * 4, par * 4 + 4)
                    hq = f"Hq{gg // 2}"
                    bp1 = f"BH1p{par}"
                    bp2 = f"BH2p{par}"
                    Hg = H[:, csl, 0:T1]
                    SCR = A2 if par == 0 else A1
                    rS = "A2" if par == 0 else "A1"
                    lnwb = bias_sb[:, 88 + gg * 4:88 + (gg + 1) * 4].unsqueeze(2).broadcast_to([128, 4, T1])
                    lnbb = bias_sb[:, 120 + gg * 4:120 + (gg + 1) * 4].unsqueeze(2).broadcast_to([128, 4, T1])
                    t_ = SCR[:, 0:4, 0:T1]
                    k.tt(t_, Hg, rb4, ALU.mult, [hq, "R1"], [rS], tag=f"p2b_n1{s}_{it}_{gg}")
                    k.tt(t_, t_, murb4, ALU.subtract, [rS, "R2"], [rS], tag=f"p2b_n2{s}_{it}_{gg}")
                    k.tt(t_, t_, lnwb, ALU.mult, [rS, "biassb"], [rS], tag=f"p2b_n3{s}_{it}_{gg}")
                    k.tt(t_, t_, lnbb, ALU.add, [rS, "biassb"], [rS], tag=f"p2b_n4{s}_{it}_{gg}")
                    k.copy(BH1[:, bsl, 0:T1], t_, [rS], [bp1], tag=f"p2b_shi{s}_{it}_{gg}")
                    k.tt(BH2[:, bsl, 0:T1], t_, BH1[:, bsl, 0:T1], ALU.subtract, [rS, bp1], [bp2], tag=f"p2b_slo{s}_{it}_{gg}")
                    for fc in range(4):
                        hc = gg * 4 + fc
                        k.dma("sync", WS[:, hc % 4, 0:1024], wpjq_d.ap()[hc * 128:(hc + 1) * 128, :],
                              [f"WSa{hc % 4}", "d_tpj"], [f"WSa{hc % 4}"], tag=f"p2b_ldwp{s}_{it}_{hc}")
                    for fc in range(4):
                        hc = gg * 4 + fc
                        mms = []
                        for dc in range(NFC):
                            wp = WS[:, hc % 4, dc * 128:(dc + 1) * 128]
                            st_ = (hc == 0)
                            sp_ = (hc == NHC - 1)
                            mms.append((pb[dc][:, 0:T1], wp, BH1[:, par * 4 + fc, 0:T1], st_, False))
                            mms.append((pb[dc][:, 0:T1], wp, BH2[:, par * 4 + fc, 0:T1], False, sp_))
                        k.mmgroup(mms, [bp1, bp2, f"WSa{hc % 4}"],
                                  [f"pb{dc2}" for dc2 in range(NFC)], tag=f"p2b_ymm{s}_{it}_{hc}")
                for dc in range(NFC):
                    k.act(A2[:, dc, 0:T1], pb[dc][:, 0:T1], ACT.Identity,
                          reads=[f"pb{dc}"], writes=["A2"], tag=f"p2b_yev{s}_{it}_{dc}")
                emit_ln(A2[:, :, 0:T1], NFC, T1, ["A2"], f"p2bln2{s}_{it}", H[:, 8:16, 0:T1], "Hq1",
                        xn=H[:, 0:NFC, 0:T1], xnres="Hq0")
                k.sch.op("gpsimd", lambda e, s=s, t0=t0: e.dma_start(
                    out=x2_d.ap()[s, :, :, t0:t0 + T1].transpose([1, 0, 2]),
                    in_=H[:, 0:NFC, 0:T1], accum_op=ALU.add),
                    ["Hq0", xdn], [xdn], dma=True, tag=f"p2b_st{s}_{it}")

    if "p3" in phases:
        def wl1(widx, part):
            off = (widx * 2 + part) * 1024
            return WB[:, off:off + NFC * C1].rearrange("p (a b) -> p a b", a=NFC)

        for widx in range(5):
            for part in range(2):
                k.dma("sync", wl1(widx, part), wl1sp.ap()[part, widx].rearrange("(a p) b -> p a b", p=128),
                      ["WB0", f"d_l1w{widx}"], ["WB0"], tag=f"p3_ldw{widx}_{part}")
        AB = k.sbuf("ABsb", [128, 16], F32)
        for s in range(NSEQ):
            for it in range(NT1):
                t0 = it * T1
                xt = A1[:, :, 0:T1]
                k.dma("sync", xt, x2_d.ap()[s, :, :, t0:t0 + T1].transpose([1, 0, 2]), ["A1", f"x2:{s}:{it}"], ["A1"], tag=f"p3_ldx{s}_{it}")
                emit_ln(xt, NFC, T1, ["A1"], f"p3ln{s}_{it}", A2[:, :, 0:T1], "A2",
                        hi=BH1[:, :, 0:T1], lo=BH2[:, :, 0:T1], hires="BH1", lores="BH2")
                for pi, (widx, bcol, dstd, sig) in enumerate([(0, 72, qfm_d, False), (1, 73, None, False), (3, 75, gfm_d, True)]):
                    bank = 2 + pi
                    mms = []
                    for kc in range(NFC):
                        lh = wl1(widx, 0)[:, kc, :]
                        ll = wl1(widx, 1)[:, kc, :]
                        mms.append((pb[bank][:, 0:T1], lh, BH1[:, kc, 0:T1]))
                        mms.append((pb[bank][:, 0:T1], lh, BH2[:, kc, 0:T1]))
                        mms.append((pb[bank][:, 0:T1], ll, BH1[:, kc, 0:T1]))
                    k.mmgroup(mms, ["BH1", "BH2", "WB0"], [f"pb{bank}"], tag=f"p3_mm{s}_{it}_{pi}")
                    dst = H[:, pi, 0:T1]
                    k.act(dst, pb[bank][:, 0:T1], ACT.Sigmoid if sig else ACT.Identity,
                          bias=bias_sb[:, bcol:bcol + 1],
                          reads=[f"pb{bank}", "biassb"], writes=["Hq0"], tag=f"p3_ev{s}_{it}_{pi}")
                    if dstd is not None:
                        nmx = "qfm" if pi == 0 else "gfm"
                        k.dma("gpsimd", dstd.ap()[s, :, t0:t0 + T1], dst, ["Hq0"], [f"{nmx}:{s}:{it}"], tag=f"p3_st{s}_{it}_{pi}")
                kraw = H[:, 1, 0:T1]
                sq = A2[:, 0, 0:T1]
                k.act(sq, kraw, ACT.Square, reads=["Hq0"], writes=["A2"], tag=f"p3_ksq{s}_{it}")
                k.mmgroup([(pb[5][0:1, 0:T1], ccol("one"), sq)], ["A2", "ccsb"], ["pb5"], tag=f"p3_kss{s}_{it}")
                nr = ST[0:1, 0:T1]
                k.ts(nr, pb[5][0:1, 0:T1], 1e-24, None, ALU.max, None, ["pb5"], ["STr"], tag=f"p3_kmx{s}_{it}")
                k.act(nr, nr, ACT.Ln, reads=["STr"], writes=["STr"], tag=f"p3_klnv{s}_{it}")
                k.ts(nr, nr, -0.5, None, ALU.mult, None, ["STr"], ["STr"], tag=f"p3_khf{s}_{it}")
                k.act(nr, nr, ACT.Exp, reads=["STr"], writes=["STr"], tag=f"p3_krc{s}_{it}")
                k.mmgroup([(pb[5][:, 0:T1], ones_sb[:], nr)], ["STr", "ones1"], ["pb5"], tag=f"p3_kbc{s}_{it}")
                kn = H[:, 1, 0:T1]
                k.tt(kn, kraw, pb[5][:, 0:T1], ALU.mult, ["Hq0", "pb5"], ["Hq0"], tag=f"p3_knm{s}_{it}")
                k.dma("gpsimd", kfm_d.ap()[s, :, t0:t0 + T1], kn, ["Hq0"], [f"kfm:{s}:{it}"], tag=f"p3_stk{s}_{it}")
                for c4 in range(T1 // 128):
                    cid = it * 4 + c4
                    csl = slice(c4 * 128, (c4 + 1) * 128)
                    mms = []
                    for kc in range(NFC):
                        mms.append((pb[6][:, 0:C1], BH1[:, kc, csl], wl1(2, 0)[:, kc, :]))
                        mms.append((pb[6][:, 0:C1], BH2[:, kc, csl], wl1(2, 0)[:, kc, :]))
                        mms.append((pb[6][:, 0:C1], BH1[:, kc, csl], wl1(2, 1)[:, kc, :]))
                    k.mmgroup(mms, ["BH1", "BH2", "WB0"], ["pb6"], tag=f"p3_vtm{s}_{it}_{c4}")
                    vtm = A2[:, 1, 0:C1]
                    k.act(vtm, pb[6][:, 0:C1], ACT.Identity, bias=bias_sb[:, 74:75],
                          reads=["pb6", "biassb"], writes=["A2"], tag=f"p3_vtev{s}_{it}_{c4}")
                    k.dma("gpsimd", vtm_d.ap()[s, cid], vtm, ["A2"], [f"vtm:{s}:{cid}"], tag=f"p3_stv{s}_{it}_{c4}")
                    mms = []
                    for kc in range(NFC):
                        mms.append((pb[7][:, 0:C1], BH1[:, kc, csl], wl1(1, 0)[:, kc, :]))
                        mms.append((pb[7][:, 0:C1], BH2[:, kc, csl], wl1(1, 0)[:, kc, :]))
                        mms.append((pb[7][:, 0:C1], BH1[:, kc, csl], wl1(1, 1)[:, kc, :]))
                    k.mmgroup(mms, ["BH1", "BH2", "WB0"], ["pb7"], tag=f"p3_ktm{s}_{it}_{c4}")
                    rcol = AB[:, 8:9]
                    k.mmgroup([(pb[4][:, 0:1], ST[0:1, c4 * 128:(c4 + 1) * 128], one11_sb[:])],
                              ["STr", "one11"], ["pb4"], tag=f"p3_rc{s}_{it}_{c4}")
                    k.copy(rcol, pb[4][:, 0:1], ["pb4"], ["ABsb"], tag=f"p3_rcev{s}_{it}_{c4}")
                    ktm = A2[:, 2, 0:C1]
                    k.act(ktm, pb[7][:, 0:C1], ACT.Identity, bias=bias_sb[:, 73:74],
                          reads=["pb7", "biassb"], writes=["A2"], tag=f"p3_ktev{s}_{it}_{c4}")
                    k.ts(ktm, ktm, rcol, None, ALU.mult, None, ["A2", "ABsb"], ["A2"], tag=f"p3_ktn{s}_{it}_{c4}")
                    k.dma("gpsimd", ktm_d.ap()[s, cid], ktm, ["A2"], [f"ktm:{s}:{cid}"], tag=f"p3_stkt{s}_{it}_{c4}")
                    mms = []
                    for kc in range(NFC):
                        mms.append((pb[4][:, 4:6], BH1[:, kc, csl], wl1(4, 0)[:, kc, 0:2]))
                        mms.append((pb[4][:, 4:6], BH2[:, kc, csl], wl1(4, 0)[:, kc, 0:2]))
                        mms.append((pb[4][:, 4:6], BH1[:, kc, csl], wl1(4, 1)[:, kc, 0:2]))
                    k.mmgroup(mms, ["BH1", "BH2", "WB0"], ["pb4"], tag=f"p3_ab{s}_{it}_{c4}")
                    zac = AB[:, 0:1]
                    zbc = AB[:, 6:7]
                    k.act(zac, pb[4][:, 4:5], ACT.Sigmoid, bias=bias_sb[:, 76:77],
                          reads=["pb4", "biassb"], writes=["ABsb"], tag=f"p3_sa{s}_{it}_{c4}")
                    k.act(zbc, pb[4][:, 5:6], ACT.Sigmoid, bias=bias_sb[:, 77:78],
                          reads=["pb4", "biassb"], writes=["ABsb"], tag=f"p3_sb{s}_{it}_{c4}")
                    k.act(AB[:, 4:5], zac, ACT.Ln, reads=["ABsb"], writes=["ABsb"], tag=f"p3_la{s}_{it}_{c4}")
                    k.act(AB[:, 5:6], zbc, ACT.Ln, reads=["ABsb"], writes=["ABsb"], tag=f"p3_lb{s}_{it}_{c4}")
                    k.dma("gpsimd", lac_d.ap()[s, cid][:, 0:3], AB[:, 4:7], ["ABsb"], [f"lac:{s}:{cid}"], tag=f"p3_stab{s}_{it}_{c4}")

    if "p4" in phases:
        NCH = SEQ // 128
        I128 = mask_sb[:, 3, :]
        TRIL_S = mask_sb[:, 0, :]
        TRIU_S = mask_sb[:, 1, :]
        TRIU_I = mask_sb[:, 2, :]
        ZROW = mask_sb[0:1, 0, :]

        def slot(i):
            return Hf[:, i * 128:(i + 1) * 128]

        SLOTMAP = {"KF": 0, "QF": 1, "GF": 2, "KT": 3, "VT": 4, "LAB": 5,
                   "RB": 12, "N": 13, "NT": 14, "Xa": 15, "XTa": 16, "Xb": 17, "XTb": 18,
                   "T1s": 19, "XVF": 20, "WBN": 21, "XVT": 22, "YF": 23, "L3m": 24,
                   "PTm": 25, "KB": 26, "OO": 27, "S0": 28, "S0T": 29, "COLS": 30, "Zt": 31}
        PERSEQ = 32

        def SL(s, name, par=0):
            i = SLOTMAP[name] + (6 * par if name in ("KF", "QF", "GF", "KT", "VT", "LAB") else 0)
            return slot(s * PERSEQ + i), f"q{s}s{i}"


        def ROW(s, off):
            return PROW[0:1, s * 512 + off:s * 512 + off + 128]

        def ROW2(s):
            return PROW[0:2, s * 512:s * 512 + 128]

        sch.fence_from(["Hq0", "Hq1", "Hq2", "Hq3", "A1", "A2", "BH1", "BH2",
                        "BH1p0", "BH1p1", "BH2p0", "BH2p1", "R1", "R2", "STr"],
                       [f"q{s}s{i}" for s in range(NSEQ) for i in range(PERSEQ)] +
                       [f"PROW{s}" for s in range(NSEQ)])
        for s in range(NSEQ):
            sm0, r0 = SL(s, "S0")
            sm1, r1 = SL(s, "S0T")
            k.sch.op("vector", lambda e, a=sm0: e.memset(a, 0.0), [], [r0], tag=f"p4_z0{s}")
            k.sch.op("vector", lambda e, a=sm1: e.memset(a, 0.0), [], [r1], tag=f"p4_z1{s}")

        def p4_load(s, j):
            par = j % 2
            csl = slice(j * 128, (j + 1) * 128)
            for name, dsrc, dn in (("KF", kfm_d.ap()[s][:, csl], f"kfm:{s}:{j // 4}"),
                                   ("QF", qfm_d.ap()[s][:, csl], f"qfm:{s}:{j // 4}"),
                                   ("GF", gfm_d.ap()[s][:, csl], f"gfm:{s}:{j // 4}"),
                                   ("KT", ktm_d.ap()[s, j], f"ktm:{s}:{j}"),
                                   ("VT", vtm_d.ap()[s, j], f"vtm:{s}:{j}")):
                ap, rn = SL(s, name, par)
                k.dma("sync", ap, dsrc, [rn, dn], [rn], tag=f"p4_ld{name}{s}_{j}")
            ap, rn = SL(s, "LAB", par)
            k.dma("sync", ap[:, 0:3], lac_d.ap()[s, j][:, 0:3], [rn, f"lac:{s}:{j}"], [rn], tag=f"p4_ldlab{s}_{j}")

        def p4_chunk(s, j):
            par = j % 2
            KF, rKF = SL(s, "KF", par)
            QF, rQF = SL(s, "QF", par)
            GF, rGF = SL(s, "GF", par)
            KT, rKT = SL(s, "KT", par)
            VT, rVT = SL(s, "VT", par)
            LAB, rLAB = SL(s, "LAB", par)
            RBs, rRB = SL(s, "RB")
            Nb, rN = SL(s, "N")
            NTb, rNT = SL(s, "NT")
            T1s, rT1 = SL(s, "T1s")
            XVF, rXVF = SL(s, "XVF")
            WBN, rWBN = SL(s, "WBN")
            XVT, rXVT = SL(s, "XVT")
            YF, rYF = SL(s, "YF")
            L3m, rL3 = SL(s, "L3m")
            PTm, rPT = SL(s, "PTm")
            KB, rKB = SL(s, "KB")
            Zt, rZt = SL(s, "Zt")
            OO, rOO = SL(s, "OO")
            S0, rS0 = SL(s, "S0")
            S0T, rS0T = SL(s, "S0T")
            COLS, rC = SL(s, "COLS")
            colA = COLS[:, 0:1]
            tcol = COLS[:, 1:2]
            A_col = COLS[:, 2:3]
            acai = COLS[:, 3:4]
            bacol = COLS[:, 4:5]
            Accol = COLS[:, 5:6]
            lgac = COLS[:, 6:7]
            ntcol = COLS[:, 7:8]
            ncolA = COLS[:, 8:9]
            bG = 2 + 3 * s
            bW = 3 + 3 * s
            bV = 4 + 3 * s
            rG = f"pb{bG}"
            rW = f"pb{bW}"
            rV = f"pb{bV}"
            J = f"{s}_{j}"
            la_row = ROW(s, 0)
            lb_row = ROW(s, 128)
            logA = ROW(s, 256)
            A_row = ROW(s, 384)
            rPROW = f"PROW{s}"
            k.mmgroup([(pb[bV][0:1, 0:128], LAB[:, 0:1], I128, True, True),
                       (pb[bV][0:1, 128:256], LAB[:, 1:2], I128, True, True)],
                      [rLAB, "masksb"], [rV], tag=f"p4_rows{J}")
            k.copy(la_row, pb[bV][0:1, 0:128], [rV], [rPROW], tag=f"p4_rowsev{J}")
            k.copy(lb_row, pb[bV][0:1, 128:256], [rV], [rPROW], tag=f"p4_rowsev2{J}")
            k.scan(logA, la_row, ZROW, 0.0, ALU.add, ALU.add, [rPROW, "masksb"], [rPROW], tag=f"p4_scan{J}")
            k.act(A_row, logA, ACT.Exp, reads=[rPROW], writes=[rPROW], tag=f"p4_arow{J}")
            k.mmgroup([(pb[bG][:, 0:128], KF, KF)], [rKF], [rG], tag=f"p4_G{J}")
            k.mmgroup([(pb[bV][:, 128:129], logA, one11_sb[:])], [rPROW, "one11"], [rV], tag=f"p4_colA{J}")
            k.copy(colA, pb[bV][:, 128:129], [rV], [rC], tag=f"p4_colAev{J}")
            k.mmgroup([(pb[bW][:, 0:128], ones_sb[:], logA)], [rPROW, "ones1"], [rW], tag=f"p4_RB{J}")
            k.copy(RBs, pb[bW][:, 0:128], [rW], [rRB], tag=f"p4_RBev{J}")
            k.tt(tcol, colA, LAB[:, 1:2], ALU.add, [rC, rLAB], [rC], tag=f"p4_tcol{J}")
            k.ts(ntcol, tcol, -1.0, None, ALU.mult, None, [rC], [rC], tag=f"p4_ntcol{J}")
            k.ts(ncolA, colA, -1.0, None, ALU.mult, None, [rC], [rC], tag=f"p4_ncolA{J}")
            k.ts(Nb, RBs, ntcol, None, ALU.add, None, [rRB, rC], [rN], tag=f"p4_D0{J}")
            k.ts(Nb, Nb, -1.0, 0.0, ALU.mult, ALU.min, [rN], [rN], tag=f"p4_D{J}")
            k.act(Nb, Nb, ACT.Exp, reads=[rN], writes=[rN], tag=f"p4_expD{J}")
            k.tt(Nb, Nb, TRIL_S, ALU.mult, [rN, "masksb"], [rN], tag=f"p4_Nm{J}")
            k.tt(Nb, Nb, pb[bG][:, 0:128], ALU.mult, [rN, rG], [rN], tag=f"p4_NG{J}")
            k.tt(lb_row, lb_row, logA, ALU.add, [rPROW], [rPROW], tag=f"p4_lab{J}")
            k.mmgroup([(pb[bW][:, 128:256], ones_sb[:], lb_row)], [rPROW, "ones1"], [rW], tag=f"p4_RB2{J}")
            k.ts(NTb, pb[bW][:, 128:256], ncolA, None, ALU.add, None, [rW, rC], [rNT], tag=f"p4_D20{J}")
            k.ts(NTb, NTb, 0.0, None, ALU.min, None, [rNT], [rNT], tag=f"p4_D2{J}")
            k.act(NTb, NTb, ACT.Exp, reads=[rNT], writes=[rNT], tag=f"p4_expD2{J}")
            k.tt(NTb, NTb, TRIU_S, ALU.mult, [rNT, "masksb"], [rNT], tag=f"p4_NTm{J}")
            k.tt(NTb, NTb, pb[bG][:, 0:128], ALU.mult, [rNT, rG], [rNT], tag=f"p4_NTG{J}")
            Xc, rXc = SL(s, "Xa")
            XTc, rXTc = SL(s, "XTa")
            Xo, rXo = SL(s, "Xb")
            XTo, rXTo = SL(s, "XTb")
            k.tt(Xc, I128, Nb, ALU.subtract, ["masksb", rN], [rXc], tag=f"p4_X0{J}")
            k.tt(XTc, I128, NTb, ALU.subtract, ["masksb", rNT], [rXTc], tag=f"p4_XT0{J}")
            for itn in range(6):
                k.mmgroup([(pb[bW][:, 256:384], NTb, Xc, True, False),
                           (pb[bW][:, 256:384], I128, Xc, False, True)],
                          [rNT, rXc, "masksb"], [rW], tag=f"p4_T1{J}_{itn}")
                k.copy(T1s, pb[bW][:, 256:384], [rW], [rT1], tag=f"p4_T1ev{J}_{itn}")
                k.mmgroup([(pb[bW][:, 384:512], XTc, T1s)], [rXTc, rT1], [rW], tag=f"p4_X2{J}_{itn}")
                k.stt(Xo, Xc, 2.0, pb[bW][:, 384:512], ALU.mult, ALU.subtract, [rXc, rW], [rXo], tag=f"p4_Xn{J}_{itn}")
                k.mmgroup([(pb[bW][:, 256:384], T1s, XTc)], [rT1, rXTc], [rW], tag=f"p4_XT2{J}_{itn}")
                k.stt(XTo, XTc, 2.0, pb[bW][:, 256:384], ALU.mult, ALU.subtract, [rXTc, rW], [rXTo], tag=f"p4_XTn{J}_{itn}")
                Xc, Xo = Xo, Xc
                rXc, rXo = rXo, rXc
                XTc, XTo = XTo, XTc
                rXTc, rXTo = rXTo, rXTc
            k.act(A_col, colA, ACT.Exp, reads=[rC], writes=[rC], tag=f"p4_acol{J}")
            k.tt(bacol, LAB[:, 2:3], A_col, ALU.mult, [rLAB, rC], [rC], tag=f"p4_bacol{J}")
            k.mmgroup([(pb[bV][:, 136:137], ones_sb[:], logA[0:1, 127:128]),
                       (pb[bV][:, 137:138], ones_sb[:], A_row[0:1, 127:128])],
                      [rPROW, "ones1"], [rV], tag=f"p4_accol{J}")
            k.copy(lgac, pb[bV][:, 136:137], [rV], [rC], tag=f"p4_lgacev{J}")
            k.copy(Accol, pb[bV][:, 137:138], [rV], [rC], tag=f"p4_accolev{J}")
            k.tt(lgac, lgac, colA, ALU.subtract, [rC], [rC], tag=f"p4_acai0{J}")
            k.act(acai, lgac, ACT.Exp, reads=[rC], writes=[rC], tag=f"p4_acai{J}")
            k.ts(VT, VT, LAB[:, 2:3], None, ALU.mult, None, [rVT, rLAB], [rVT], tag=f"p4_Vb{J}")
            k.ts(KB, KT, bacol, None, ALU.mult, None, [rKT, rC], [rKB], tag=f"p4_Kba{J}")
            k.mmgroup([(pb[bW][:, 0:128], VT, XTc)], [rVT, rXTc], [rW], tag=f"p4_XVF{J}")
            k.copy(XVF, pb[bW][:, 0:128], [rW], [rXVF], tag=f"p4_XVFev{J}")
            k.mmgroup([(pb[bW][:, 128:256], KB, XTc)], [rKB, rXTc], [rW], tag=f"p4_WBF{J}")
            k.ts(WBN, pb[bW][:, 128:256], -1.0, None, ALU.mult, None, [rW], [rWBN], tag=f"p4_WBN{J}")
            k.mmgroup([(pb[bW][:, 256:384], XTc, VT)], [rXTc, rVT], [rW], tag=f"p4_XVT{J}")
            k.copy(XVT, pb[bW][:, 256:384], [rW], [rXVT], tag=f"p4_XVTev{J}")
            k.mmgroup([(pb[bG][:, 128:256], S0, QF)], [rS0, rQF], [rG], tag=f"p4_YF{J}")
            k.copy(YF, pb[bG][:, 128:256], [rG], [rYF], tag=f"p4_YFev{J}")
            k.mmgroup([(pb[bW][:, 384:512], XVF, QF, True, False),
                       (pb[bW][:, 384:512], WBN, YF, False, True)],
                      [rXVF, rWBN, rQF, rYF], [rW], tag=f"p4_PT{J}")
            k.ts(L3m, RBs, ncolA, None, ALU.add, None, [rRB, rC], [rL3], tag=f"p4_D30{J}")
            k.ts(L3m, L3m, 0.0, None, ALU.min, None, [rL3], [rL3], tag=f"p4_D3{J}")
            k.act(L3m, L3m, ACT.Exp, reads=[rL3], writes=[rL3], tag=f"p4_expD3{J}")
            k.tt(L3m, L3m, TRIU_I, ALU.mult, [rL3, "masksb"], [rL3], tag=f"p4_L3mm{J}")
            k.tt(PTm, pb[bW][:, 384:512], L3m, ALU.mult, [rW, rL3], [rPT], tag=f"p4_PTm{J}")
            k.mmgroup([(pb[bG][:, 256:384], KT, PTm)], [rKT, rPT], [rG], tag=f"p4_O2{J}")
            k.mmgroup([(pb[bV][:, 256:384], ones_sb[:], A_row)], [rPROW, "ones1"], [rV], tag=f"p4_ABB{J}")
            k.tt(OO, YF, pb[bV][:, 256:384], ALU.mult, [rYF, rV], [rOO], tag=f"p4_Ys{J}")
            k.tt(OO, OO, pb[bG][:, 256:384], ALU.add, [rOO, rG], [rOO], tag=f"p4_O{J}")
            k.tt(OO, OO, GF, ALU.mult, [rOO, rGF], [rOO], tag=f"p4_Og{J}")
            k.dma("gpsimd", ofm_d.ap()[s][:, j * 128:(j + 1) * 128], OO, [rOO], [f"ofm:{s}:{j}"], tag=f"p4_stO{J}")
            k.mmgroup([(pb[bW][:, 0:128], WBN, S0T)], [rWBN, rS0T], [rW], tag=f"p4_WS{J}")
            k.tt(Zt, XVT, pb[bW][:, 0:128], ALU.add, [rXVT, rW], [rZt], tag=f"p4_Zt{J}")
            k.ts(Zt, Zt, acai, None, ALU.mult, None, [rZt, rC], [rZt], tag=f"p4_Z{J}")
            k.mmgroup([(pb[bW][:, 128:256], Zt, KT)], [rZt, rKT], [rW], tag=f"p4_S0n{J}")
            k.mmgroup([(pb[bW][:, 256:384], KT, Zt)], [rKT, rZt], [rW], tag=f"p4_S0Tn{J}")
            k.stt(S0, S0, Accol, pb[bW][:, 128:256], ALU.mult, ALU.add, [rS0, rC, rW], [rS0], tag=f"p4_S0{J}")
            k.stt(S0T, S0T, Accol, pb[bW][:, 256:384], ALU.mult, ALU.add, [rS0T, rC, rW], [rS0T], tag=f"p4_S0T{J}")
            if s == 0 and j == 0 and "p4dbg" in dbg:
                pd = k.dram_t("p4dump", [8, 128, 128], F32, kind="ExternalOutput")
                k.dma("gpsimd", pd.ap()[0][:, 0:4], LAB[:, 0:4], [rLAB], [], tag="p4d_lab")
                k.dma("gpsimd", pd.ap()[1][0:2, :], PROW[0:2, 0:128], [rPROW], [], tag="p4d_rows0")
                k.dma("gpsimd", pd.ap()[1][2:3, :], PROW[0:1, 256:384], [rPROW], [], tag="p4d_loga")
                k.dma("gpsimd", pd.ap()[1][3:4, :], PROW[0:1, 384:512], [rPROW], [], tag="p4d_arow")
                k.dma("gpsimd", pd.ap()[2], Nb, [rN], [], tag="p4d_N")
                k.dma("gpsimd", pd.ap()[3], NTb, [rNT], [], tag="p4d_NT")
                k.dma("gpsimd", pd.ap()[4], XTc, [rXTc], [], tag="p4d_XT")
                k.dma("gpsimd", pd.ap()[5], PTm, [rPT], [], tag="p4d_PTm")
                k.dma("gpsimd", pd.ap()[6], OO, [rOO], [], tag="p4d_OO")
                k.dma("gpsimd", pd.ap()[7][:, 0:8], COLS[:, 0:8], [rC], [], tag="p4d_cols")

        p4_load(0, 0)
        p4_load(1, 0)
        for j in range(NCH):
            for s in range(NSEQ):
                if j + 1 < NCH:
                    p4_load(s, j + 1)
                p4_chunk(s, j)

    if "p5" in phases:
        sch.fence_from([f"q{s}s{i}" for s in range(NSEQ) for i in range(32)],
                       ["Hq0", "Hq1", "Hq2", "Hq3"])
        sch.fence_from(["PROW0", "PROW1"], ["PROW0"])
        WB8 = WB[:].bitcast(FP8)
        w1v = WB8[:, 0:NFC * HFF].rearrange("p (a b) -> p a b", a=NFC)
        w2v = WB8[:, NFC * HFF:2 * NFC * HFF].rearrange("p (a b) -> p a b", a=NHC)
        k.dma("sync", w1v, w1q_d.ap().rearrange("(a p) b -> p a b", p=128),
              ["WB0", "WB1", "d_qw1"], ["WB0", "WB1"], tag="p5_ldw1")
        k.dma("sync", w2v, w2q_d.ap().rearrange("(a p) b -> p a b", p=128),
              ["WB2", "WB3", "d_qw2"], ["WB2", "WB3"], tag="p5_ldw2")
        WSv = WS[:, :, 0:1024]
        TILES = [(s, it) for s in range(NSEQ) for it in range(NT1)]

        def p5_headA(ti):
            s, it = TILES[ti]
            t0 = it * T1
            sl = 2 * (ti % 2)
            ofns = [f"ofm:{s}:{it * 4 + c}" for c in range(4)]
            osb = H[:, 0, 0:T1]
            k.dma("sync", osb, ofm_d.ap()[s][:, t0:t0 + T1], ["Hq0"] + ofns, ["Hq0"], tag=f"p5_ldo{s}_{it}")
            k.dma("sync", WSv[:, sl, :], wl1osp.ap()[0].rearrange("(a p) b -> p a b", p=128)[:, 0, :],
                  [f"WSa{sl}", "d_l1wo"], [f"WSa{sl}"], tag=f"p5_ldwo0{s}_{it}")
            k.dma("sync", WSv[:, sl + 1, :], wl1osp.ap()[1].rearrange("(a p) b -> p a b", p=128)[:, 0, :],
                  [f"WSa{sl + 1}", "d_l1wo"], [f"WSa{sl + 1}"], tag=f"p5_ldwo1{s}_{it}")
            k.copy(BH1[:, 0, 0:T1], osb, ["Hq0"], ["BH1p0"], tag=f"p5_ohi{s}_{it}")
            k.tt(BH1[:, 1, 0:T1], osb, BH1[:, 0, 0:T1], ALU.subtract, ["Hq0", "BH1p0"], ["BH1p0"], tag=f"p5_olo{s}_{it}")
            for oc in range(NFC):
                bank = 2 + oc % 4
                lh = WSv[:, sl, oc * 128:(oc + 1) * 128]
                ll = WSv[:, sl + 1, oc * 128:(oc + 1) * 128]
                mms = [(pb[bank][:, 0:T1], lh, BH1[:, 0, 0:T1]),
                       (pb[bank][:, 0:T1], lh, BH1[:, 1, 0:T1]),
                       (pb[bank][:, 0:T1], ll, BH1[:, 0, 0:T1])]
                k.mmgroup(mms, ["BH1p0", f"WSa{sl}", f"WSa{sl + 1}"], [f"pb{bank}"], tag=f"p5_omm{s}_{it}_{oc}")
                k.act(H[:, 8 + oc, 0:T1], pb[bank][:, 0:T1], ACT.Identity, bias=bias_sb[:, 80 + oc:81 + oc],
                      reads=[f"pb{bank}", "biassb"], writes=["Hq1"], tag=f"p5_oev{s}_{it}_{oc}")

        def p5_headB(ti):
            s, it = TILES[ti]
            t0 = it * T1
            x2n = f"x2:{s}:{it}"
            x3n = f"x3:{s}:{it}"
            xon = f"xo:{s}:{it}"
            emit_ln(H[:, 8:16, 0:T1], NFC, T1, ["Hq1"], f"p5ln1{s}_{it}", H[:, 24:32, 0:T1], "Hq3",
                    xn=H[:, 8:16, 0:T1], xnres="Hq1", psA=6, psB=7, par=1)
            k.dma("scalar", x3_d.ap()[s, :, :, t0:t0 + T1], x2_d.ap()[s, :, :, t0:t0 + T1],
                  [x3n, x2n], [x3n], tag=f"p5_pc3{s}_{it}")
            k.sch.op("gpsimd", lambda e, s=s, t0=t0: e.dma_start(
                out=x3_d.ap()[s, :, :, t0:t0 + T1].transpose([1, 0, 2]),
                in_=H[:, 8:16, 0:T1], accum_op=ALU.add),
                ["Hq1", x3n], [x3n], dma=True, tag=f"p5_st3{s}_{it}")
            k.dma("scalar", xout.ap()[s, :, :, t0:t0 + T1], x3_d.ap()[s, :, :, t0:t0 + T1],
                  [xon, x3n], [xon], tag=f"p5_pco{s}_{it}")

        def p5_body(ti):
            s, it = TILES[ti]
            t0 = it * T1
            x2n = f"x2:{s}:{it}"
            xt = A1[:, :, 0:T1]
            k.dma("sync", xt, x2_d.ap()[s, :, :, t0:t0 + T1].transpose([1, 0, 2]), ["A1", x2n], ["A1"], tag=f"p5_ldx{s}_{it}")
            k.dma("sync", BH2[:, :, 0:T1], dmask_in.ap()[s, :, :, t0:t0 + T1].transpose([1, 0, 2]),
                  ["BH2p0", "BH2p1"], ["BH2p0", "BH2p1"], tag=f"p5_ldm{s}_{it}")
            k.tt(xt, xt, H[:, 8:16, 0:T1], ALU.add, ["A1", "Hq1"], ["A1"], tag=f"p5_res1{s}_{it}")
            emit_ln(xt, NFC, T1, ["A1"], f"p5ln2{s}_{it}", A2[:, :, 0:T1], "A2",
                    xn=H[:, 16:24, 0:T1], xnres="Hq2", psA=6, psB=7, par=1)
            xn3 = H[:, 16:24, 0:T1]
            t1q = A2[:, :, 0:T1]
            k.act(t1q, xn3, ACT.Abs, reads=["Hq2"], writes=["A2"], tag=f"p5_qa{s}_{it}")
            k.act(t1q, t1q, ACT.Ln, bias=ccol("tiny"), reads=["A2", "ccsb"], writes=["A2"], tag=f"p5_ql{s}_{it}")
            k.ts(t1q, t1q, 1.0 / LN2, -0.5, ALU.mult, ALU.add, ["A2"], ["A2"], tag=f"p5_qv{s}_{it}")
            k.ts(t1q, t1q, dcol("bcol"), dcol("mbcol"), ALU.add, ALU.add, ["A2", "dcsb"], ["A2"], tag=f"p5_qe{s}_{it}")
            k.ts(t1q, t1q, -6.0, 8.0, ALU.max, ALU.min, ["A2"], ["A2"], tag=f"p5_qc{s}_{it}")
            k.act(t1q, t1q, ACT.Exp, bias=dcol("bs4"), scale=LN2, reads=["A2", "dcsb"], writes=["A2"], tag=f"p5_qb{s}_{it}")
            k.tt(xn3, xn3, t1q, ALU.add, ["Hq2", "A2"], ["Hq2"], tag=f"p5_qu{s}_{it}")
            k.tt(BQ8[:, 0, :, 0:T1], xn3, t1q, ALU.subtract, ["Hq2", "A2"], ["BQ8a"], tag=f"p5_qq{s}_{it}")
            for hc in range(NHC):
                bank = 2 + hc % 2
                mms = [(pb[bank][:, 0:T1], w1v[:, kc, hc * 128:(hc + 1) * 128], BQ8[:, 0, kc, 0:T1])
                       for kc in range(NFC)]
                k.mmgroup(mms, ["BQ8a", "WB0"], [f"pb{bank}"], tag=f"p5_hmm{s}_{it}_{hc}")
                k.act(H[:, hc, 0:T1], pb[bank][:, 0:T1], ACT.Identity, bias=bias_sb[:, 32 + hc:33 + hc],
                      reads=[f"pb{bank}", "biassb"], writes=[f"Hq{hc // 8}"], tag=f"p5_hev{s}_{it}_{hc}")
            for gg in range(4):
                csl = slice(gg * 8, (gg + 1) * 8)
                hq = ["Hq0", "Hq1", "Hq2", "Hq3"][gg]
                Hg = H[:, csl, 0:T1]
                srn = "A2" if gg % 2 == 0 else "A1"
                t1_ = (A2 if gg % 2 == 0 else A1)[:, :, 0:T1]
                k.act(t1_, Hg, ACT.Abs, reads=[hq], writes=[srn], tag=f"p5_h1a{s}_{it}_{gg}")
                k.act(t1_, t1_, ACT.Ln, bias=ccol("tiny"), reads=[srn, "ccsb"], writes=[srn], tag=f"p5_h1l{s}_{it}_{gg}")
                k.ts(t1_, t1_, 1.0 / LN2, -0.5, ALU.mult, ALU.add, [srn], [srn], tag=f"p5_h1v{s}_{it}_{gg}")
                k.ts(t1_, t1_, dcol("bcol"), dcol("mbcol"), ALU.add, ALU.add, [srn, "dcsb"], [srn], tag=f"p5_h1e{s}_{it}_{gg}")
                k.ts(t1_, t1_, -14.0, 8.0, ALU.max, ALU.min, [srn], [srn], tag=f"p5_h1c{s}_{it}_{gg}")
                k.act(t1_, t1_, ACT.Exp, bias=dcol("bs5"), scale=LN2, reads=[srn, "dcsb"], writes=[srn], tag=f"p5_h1b{s}_{it}_{gg}")
                k.stt(Hg, Hg, 0.0, t1_, ALU.max, ALU.add, [hq, srn], [hq], tag=f"p5_h1u{s}_{it}_{gg}")
                k.tt(Hg, Hg, t1_, ALU.subtract, [hq, srn], [hq], tag=f"p5_h1q{s}_{it}_{gg}")
            for gg in range(4):
                par = gg % 2
                csl = slice(gg * 8, (gg + 1) * 8)
                hq = ["Hq0", "Hq1", "Hq2", "Hq3"][gg]
                BQ = BQ8[:, 0, :, :] if par == 0 else BQ8[:, 1, :, :]
                bq_r = ["BQ8a"] if par == 0 else ["BQ8b"]
                Hg = H[:, csl, 0:T1]
                srn = "A2" if gg % 2 == 0 else "A1"
                t1_ = (A2 if gg % 2 == 0 else A1)[:, :, 0:T1]
                k.act(t1_, Hg, ACT.Ln, bias=ccol("tiny"), reads=[hq, "ccsb"], writes=[srn], tag=f"p5_h2l{s}_{it}_{gg}")
                k.ts(t1_, t1_, 1.0 / LN2, -0.5, ALU.mult, ALU.add, [srn], [srn], tag=f"p5_h2v{s}_{it}_{gg}")
                k.ts(t1_, t1_, dcol("bcol"), dcol("mbcol"), ALU.add, ALU.add, [srn, "dcsb"], [srn], tag=f"p5_h2e{s}_{it}_{gg}")
                k.ts(t1_, t1_, -6.0, 8.0, ALU.max, ALU.min, [srn], [srn], tag=f"p5_h2c{s}_{it}_{gg}")
                k.act(t1_, t1_, ACT.Exp, bias=ccol("b8ln2"), scale=-LN2, reads=[srn, "ccsb"], writes=[srn], tag=f"p5_h2p{s}_{it}_{gg}")
                k.ts(t1_, t1_, dcol("bcol"), dcol("mbcol"), ALU.add, ALU.add, [srn, "dcsb"], [srn], tag=f"p5_h2pe{s}_{it}_{gg}")
                k.stt(Hg, Hg, 2.0 ** -5, t1_, ALU.mult, ALU.mult, [hq, srn], [hq], tag=f"p5_h2t{s}_{it}_{gg}")
                k.ts(Hg, Hg, dcol("bcol"), dcol("mbcol"), ALU.add, ALU.add, [hq, "dcsb"], [hq], tag=f"p5_h2n{s}_{it}_{gg}")
                k.act(t1_, t1_, ACT.Ln, reads=[srn], writes=[srn], tag=f"p5_h2rl{s}_{it}_{gg}")
                k.ts(t1_, t1_, -1.0, 14.0 * LN2, ALU.mult, ALU.add, [srn], [srn], tag=f"p5_h2rn{s}_{it}_{gg}")
                k.act(t1_, t1_, ACT.Exp, reads=[srn], writes=[srn], tag=f"p5_h2rx{s}_{it}_{gg}")
                k.ts(t1_, t1_, dcol("bcol"), dcol("mbcol"), ALU.add, ALU.add, [srn, "dcsb"], [srn], tag=f"p5_h2se{s}_{it}_{gg}")
                k.stt(BQ[:, :, 0:T1], Hg, 2.0 ** -9, t1_, ALU.mult, ALU.mult, [hq, srn], bq_r, tag=f"p5_h2q{s}_{it}_{gg}")
                for fc in range(NFC):
                    hc = gg * 8 + fc
                    mms = []
                    for dc in range(NFC):
                        wp = w2v[:, hc, dc * 128:(dc + 1) * 128]
                        mms.append((pb[dc][:, 0:T1], wp, BQ[:, fc, 0:T1], hc == 0, hc == NHC - 1))
                    k.mmgroup(mms, bq_r + ["WB2"],
                              [f"pb{dc2}" for dc2 in range(NFC)], tag=f"p5_ymm{s}_{it}_{hc}")
            for dc in range(NFC):
                k.act(A2[:, dc, 0:T1], pb[dc][:, 0:T1], ACT.Identity, bias=bias_sb[:, 64 + dc:65 + dc],
                      reads=[f"pb{dc}", "biassb"], writes=["A2"], tag=f"p5_yev{s}_{it}_{dc}")

        def p5_tailA(ti):
            s, it = TILES[ti]
            t1y = H[:, 16:24, 0:T1]
            k.act(t1y, A2[:, :, 0:T1], ACT.Abs, reads=["A2"], writes=["Hq2"], tag=f"p5_y1a{s}_{it}")
            k.act(t1y, t1y, ACT.Ln, bias=ccol("tiny"), reads=["Hq2", "ccsb"], writes=["Hq2"], tag=f"p5_y1l{s}_{it}")
            k.ts(t1y, t1y, 1.0 / LN2, -0.5, ALU.mult, ALU.add, ["Hq2"], ["Hq2"], tag=f"p5_y1v{s}_{it}")
            k.ts(t1y, t1y, dcol("bcol"), dcol("mbcol"), ALU.add, ALU.add, ["Hq2", "dcsb"], ["Hq2"], tag=f"p5_y1e{s}_{it}")
            k.ts(t1y, t1y, -14.0, 8.0, ALU.max, ALU.min, ["Hq2"], ["Hq2"], tag=f"p5_y1c{s}_{it}")
            k.act(t1y, t1y, ACT.Exp, bias=dcol("bs5"), scale=LN2, reads=["Hq2", "dcsb"], writes=["Hq2"], tag=f"p5_y1b{s}_{it}")
            k.tt(A2[:, :, 0:T1], A2[:, :, 0:T1], t1y, ALU.add, ["A2", "Hq2"], ["A2"], tag=f"p5_y1u{s}_{it}")
            k.tt(A2[:, :, 0:T1], A2[:, :, 0:T1], t1y, ALU.subtract, ["A2", "Hq2"], ["A2"], tag=f"p5_y1q{s}_{it}")

        def p5_tailB(ti):
            s, it = TILES[ti]
            t0 = it * T1
            xon = f"xo:{s}:{it}"
            k.copy(A1[:, :, 0:T1], BH2[:, :, 0:T1], ["BH2p0", "BH2p1"], ["A1"], tag=f"p5_mcp{s}_{it}")
            k.tt(A2[:, :, 0:T1], A2[:, :, 0:T1], A1[:, :, 0:T1], ALU.mult, ["A2", "A1"], ["A2"], tag=f"p5_msk{s}_{it}")
            emit_ln(A2[:, :, 0:T1], NFC, T1, ["A2"], f"p5ln3{s}_{it}", H[:, 16:24, 0:T1], "Hq2",
                    xn=A2[:, :, 0:T1], xnres="A2", psA=0, psB=1, par=0)
            k.sch.op("gpsimd", lambda e, s=s, t0=t0: e.dma_start(
                out=xout.ap()[s, :, :, t0:t0 + T1].transpose([1, 0, 2]),
                in_=A2[:, :, 0:T1], accum_op=ALU.add),
                ["A2", xon], [xon], dma=True, tag=f"p5_sto{s}_{it}")

        NTILES = len(TILES)
        p5_headA(0)
        p5_headB(0)
        for ti in range(NTILES):
            p5_body(ti)
            if ti + 1 < NTILES:
                p5_headA(ti + 1)
                p5_headB(ti + 1)
            p5_tailA(ti)
            p5_tailB(ti)

    if "kvdump" in dbg:
        kv_dump = k.dram_t("kvdump", [128, 4 * NFC], F32, kind="ExternalOutput")
        dn = k.dma("gpsimd", kv_dump.ap(), accsb[:, 0:4 * NFC], ["accsb"], [], tag="kvdump")
        sch.barrier_end(dn)

    if "copyout" in phases:
        for s2 in range(NSEQ):
            dn = k.dma("gpsimd", xout.ap()[s2], x_in.ap()[s2], [], [], tag=f"cpout{s2}")
            sch.barrier_end(dn)

    with nc.Block() as block:
        sch.finalize(block)
    stack.close()
    return nc


def host_masks():
    return np.stack([np.tril(np.ones((128, 128), np.float32), -1),
                     np.triu(np.ones((128, 128), np.float32), 1),
                     np.triu(np.ones((128, 128), np.float32), 0),
                     np.eye(128, dtype=np.float32)])


BIAS = {"bq0": 0, "bk0": 8, "bv0": 16, "bo0": 24, "b1": 32, "b2": 64,
        "bq1": 72, "bk1": 73, "bv1": 74, "bg1": 75, "ba1": 76, "bb1": 77,
        "bo1": 80, "lnw": 88, "lnb": 120}


def host_biases(inp):
    bias = np.zeros((128, 160), np.float32)
    bias[:, 0:8] = np.asarray(inp["L0_attn_bq"]).reshape(8, 128).T
    bias[:, 8:16] = np.asarray(inp["L0_attn_bk"]).reshape(8, 128).T
    bias[:, 16:24] = np.asarray(inp["L0_attn_bv"]).reshape(8, 128).T
    bias[:, 24:32] = np.asarray(inp["L0_attn_bo"]).reshape(8, 128).T
    bias[:, 32:64] = np.asarray(inp["L1_ffn_b1"]).reshape(32, 128).T
    bias[:, 64:72] = np.asarray(inp["L1_ffn_b2"]).reshape(8, 128).T
    bias[:, 72:73] = np.asarray(inp["L1_attn_bq"]).reshape(1, 128).T
    bias[:, 73:74] = np.asarray(inp["L1_attn_bk"]).reshape(1, 128).T
    bias[:, 74:75] = np.asarray(inp["L1_attn_bv"]).reshape(1, 128).T
    bias[:, 75:76] = np.asarray(inp["L1_attn_bg"]).reshape(1, 128).T
    bias[:, 76] = float(np.asarray(inp["L1_attn_ba"])[0])
    bias[:, 77] = float(np.asarray(inp["L1_attn_bb"])[0])
    bias[0, 78] = float(np.asarray(inp["L1_attn_ba"])[0])
    bias[1, 78] = float(np.asarray(inp["L1_attn_bb"])[0])
    bias[:, 80:88] = np.asarray(inp["L1_attn_bo"]).reshape(8, 128).T
    bias[:, 88:120] = np.asarray(inp["L0_ffn_ln_w"]).reshape(32, 128).T
    bias[:, 120:152] = np.asarray(inp["L0_ffn_ln_b"]).reshape(32, 128).T
    return bias


def _fm(x):
    return np.ascontiguousarray(np.asarray(x).T.reshape(NFC, 128, x.shape[0]))


def host_inputs(inp, phases):
    import ml_dtypes
    cc = host_consts()
    masks = host_masks()
    biases = host_biases(inp)
    ones_row = np.ones((1, 128), np.float32)
    one11 = np.ones((1, 1), np.float32)
    x = np.asarray(inp["L0_attn_x"])
    w_l0qkv = np.stack([np.asarray(inp["L0_attn_wq"]).reshape(D, D),
                        np.asarray(inp["L0_attn_wk"]).reshape(D, D),
                        np.asarray(inp["L0_attn_wv"]).reshape(D, D)]).astype(np.float32)
    w_l0o = np.ascontiguousarray(np.asarray(inp["L0_attn_wo"]).reshape(D, D))
    w_fcT = np.ascontiguousarray(np.asarray(inp["L0_ffn_W_fc"]).T)
    w_pjT = np.ascontiguousarray(np.asarray(inp["L0_ffn_W_proj"]).T)
    wab = np.zeros((D, C1), np.float32)
    wab[:, 0:1] = np.asarray(inp["L1_attn_Wa"])
    wab[:, 1:2] = np.asarray(inp["L1_attn_Wb"])
    w_l1p = np.stack([np.asarray(inp["L1_attn_Wq"]), np.asarray(inp["L1_attn_Wk"]),
                      np.asarray(inp["L1_attn_Wv"]), np.asarray(inp["L1_attn_Wg"]), wab]).astype(np.float32)
    w_l1o = np.ascontiguousarray(np.asarray(inp["L1_attn_Wo"]))
    w_1T = np.ascontiguousarray(np.asarray(inp["L1_ffn_w1"]).T)
    w_2T = np.ascontiguousarray(np.asarray(inp["L1_ffn_w2"]).T)
    import jax
    _pf = jax.config.jax_threefry_partitionable
    jax.config.update("jax_threefry_partitionable", True)
    try:
        with jax.default_device(jax.devices("cpu")[0]):
            keep = np.asarray(jax.random.bernoulli(
                jax.random.key(42, impl="threefry2x32"), 0.8, (B, SEQ, D)))
    finally:
        jax.config.update("jax_threefry_partitionable", _pf)
    dmask_full = np.where(keep, np.float32(1.25), np.float32(0.0))
    in_maps = []
    for core in range(8):
        s0 = 2 * core
        xs = np.stack([_fm(x[s0]), _fm(x[s0 + 1])])
        dmc = np.stack([_fm(dmask_full[s0]), _fm(dmask_full[s0 + 1])]).astype(ml_dtypes.bfloat16)
        in_maps.append({
            "x": xs, "cc": cc, "ones_row": ones_row, "one_11": one11,
            "masks": masks, "biases": biases, "dmask": dmc,
            "w_l0qkv": w_l0qkv, "w_l0o": w_l0o, "w_fcT": w_fcT, "w_pjT": w_pjT,
            "w_l1p": w_l1p, "w_l1o": w_l1o, "w_1T": w_1T, "w_2T": w_2T,
        })
    return in_maps


ALL_PHASES = ["p0w", "p0t", "p0q", "p1", "p2a", "p2b", "p3", "p4", "p5"]


def kernel(**inputs):
    inp = {kk: np.asarray(v) for kk, v in inputs.items()}
    nc = build_program(ALL_PHASES)
    in_maps = host_inputs(inp, ALL_PHASES)
    res = run_bass_kernel_spmd(nc, in_maps, core_ids=list(range(8)))
    out = np.empty((B, SEQ, D), np.float32)
    for core in range(8):
        xo = np.asarray(res.results[core]["xout"])
        for s in range(NSEQ):
            out[2 * core + s] = xo[s].reshape(D, SEQ).T
    return out


# version 26 (final)
# speedup vs baseline: 1.0187x; 1.0187x over previous
import contextlib
import math

import numpy as np
import concourse.bass as bass
import concourse.mybir as mybir
from concourse.bass_utils import run_bass_kernel_spmd

F32 = mybir.dt.float32
FP8 = mybir.dt.float8e4
BF16 = mybir.dt.bfloat16
ALU = mybir.AluOpType
ACT = mybir.ActivationFunctionType
AXX = mybir.AxisListType.X
LN2 = float(np.log(2.0))

ENGINES = ("sync", "scalar", "vector", "tensor", "gpsimd")

B, SEQ, D = 16, 4096, 1024
NSEQ = 2
HFF = 4096
C1 = 128
NFC = 8
NHC = 32
B32 = float(np.float32(1.5 * 2.0 ** 23))
B64 = float(np.float32(1.5 * 2.0 ** 52))


NPOOL = 8


class Sched:

    def __init__(self, nc, stack):
        self.nc = nc
        self.ops = {e: [] for e in ENGINES}
        self.serial = {}
        self.writes = {}
        self.readers = {}
        self.sems = {}
        for e in ENGINES:
            n = f"{e}_x"
            self.sems[n] = stack.enter_context(nc.semaphore(n))
            self.serial[n] = 0
        self.dmaq = {}
        for e in ("sync", "gpsimd", "scalar", "vector"):
            pool = []
            for j in range(NPOOL):
                n = f"{e}_d{j}"
                self.sems[n] = stack.enter_context(nc.semaphore(n))
                pool.append(n)
            self.dmaq[e] = {"pool": pool, "cum": {n: 0 for n in pool}, "nepoch": 0,
                            "open": None, "epochs": {}}

    def _close_epoch(self, q):
        st = self.dmaq[q]
        if st["open"] is not None:
            ep = st["epochs"][st["open"]]
            ep["closed"] = True
            st["cum"][ep["sem"]] += 16 * ep["n"]
            ep["target"] = st["cum"][ep["sem"]]
            st["open"] = None

    def _resolve(self, tok):
        if tok[0] == "E":
            _, q, eid = tok
            st = self.dmaq[q]
            ep = st["epochs"][eid]
            if not ep["closed"]:
                assert st["open"] == eid
                self._close_epoch(q)
            return ep["sem"], ep["target"]
        return tok

    def op(self, eng, fn, reads=(), writes=(), dma=False, tag=""):
        wl = {}

        def addw(tok):
            sn, sv = self._resolve(tok)
            if wl.get(sn, 0) < sv:
                wl[sn] = sv

        for r in reads:
            for tok in self.writes.get(r, {}).values():
                addw(tok)
        for w in writes:
            for tok in self.readers.get(w, {}).values():
                addw(tok)
            for tok in self.writes.get(w, {}).values():
                addw(tok)

        if dma:
            st = self.dmaq[eng]
            if st["open"] is None:
                sem = st["pool"][st["nepoch"] % NPOOL]
                eid = st["nepoch"]
                st["nepoch"] += 1
                st["open"] = eid
                st["epochs"][eid] = {"sem": sem, "n": 0, "closed": False, "target": None}
                if st["cum"][sem] > 0 and wl.get(sem, 0) < st["cum"][sem]:
                    wl[sem] = st["cum"][sem]
            ep = st["epochs"][st["open"]]
            ep["n"] += 1
            done_tok = ("E", eng, st["open"])
            incsem = ep["sem"]
        else:
            semname = f"{eng}_x"
            self.serial[semname] += 1
            done_tok = (semname, self.serial[semname])
            incsem = semname

        waits = list(wl.items())
        self.ops[eng].append((waits, fn, done_tok, incsem, tag))
        key = done_tok if done_tok[0] == "E" else done_tok[0]
        for w in writes:
            self.writes[w] = {key: done_tok}
            self.readers[w] = {}
        for r in reads:
            self.readers.setdefault(r, {})[key] = done_tok
        return done_tok

    def fence_from(self, srcs, dsts):
        tok = {}
        for b in srcs:
            for kk2, t in self.writes.get(b, {}).items():
                tok[kk2] = t
            for kk2, t in self.readers.get(b, {}).items():
                if kk2 in tok and kk2.endswith("_x") if isinstance(kk2, str) else False:
                    if t[1] > tok[kk2][1]:
                        tok[kk2] = t
                else:
                    tok[kk2] = t
        for d in dsts:
            self.writes[d] = dict(tok)
            self.readers[d] = {}

    def barrier_end(self, done):
        pass

    def finalize(self, block):
        for q in self.dmaq:
            self._close_epoch(q)
        needed = {f"{e}_x": set() for e in ENGINES}
        for e in ENGINES:
            for waits, _, _, _, _ in self.ops[e]:
                for sn, sv in waits:
                    if sn.endswith("_x"):
                        needed[sn].add(sv)
        value_of = {}
        for sn, ss in needed.items():
            value_of[sn] = {s: v for v, s in enumerate(sorted(ss), start=1)}

        def wval(sn, sv):
            if sn.endswith("_x"):
                m = value_of[sn]
                if sv in m:
                    return m[sv]
                cand = [s for s in m if s <= sv]
                return m[max(cand)] if cand else 0
            return sv

        fw = {}
        for q, st in self.dmaq.items():
            for sem, cum in st["cum"].items():
                if cum > 0:
                    fw[sem] = cum
        lastwait = {}

        def engine_body(e):
            def run(eng):
                for waits, fn, done_tok, incsem, tag in self.ops[e]:
                    for sn, sv in waits:
                        val = wval(sn, sv)
                        if val <= 0 or lastwait.get((e, sn), -1) >= val:
                            continue
                        lastwait[(e, sn)] = val
                        eng.wait_ge(self.sems[sn], val)
                    ins = fn(eng)
                    if incsem.endswith("_x"):
                        if done_tok[1] in needed[incsem]:
                            assert ins is not None, f"op needs inc but returned None: {tag}"
                            ins.then_inc(self.sems[incsem], 1)
                    else:
                        assert ins is not None, f"dma op returned None: {tag}"
                        ins.then_inc(self.sems[incsem], 16)
                if e == "vector":
                    for sn, val in sorted(fw.items()):
                        if lastwait.get((e, sn), -1) < val:
                            eng.wait_ge(self.sems[sn], val)
            return run

        for e in ENGINES:
            getattr(block, e)(engine_body(e))


class K:
    def __init__(self, nc, stack, sch):
        self.nc = nc
        self.stack = stack
        self.sch = sch
        self.sb = {}
        self.ps = {}
        self.dram = {}

    def sbuf(self, name, shape, dtype):
        t = self.stack.enter_context(self.nc.sbuf_tensor(name, list(shape), dtype))
        self.sb[name] = t
        return t

    def psum(self, name):
        t = self.stack.enter_context(self.nc.psum_tensor(name, [128, 512], F32))
        self.ps[name] = t
        return t

    def dram_t(self, name, shape, dtype, kind="Internal"):
        t = self.nc.dram_tensor(name, list(shape), dtype, kind=kind)
        self.dram[name] = t
        return t

    def dma(self, eng, out, in_, reads, writes, tag=""):
        return self.sch.op(eng, lambda e: e.dma_start(out=out, in_=in_), reads, writes, dma=True, tag=tag)

    def tt(self, out, a, b, op, reads, writes, eng="vector", tag=""):
        return self.sch.op(eng, lambda e: e.tensor_tensor(out=out, in0=a, in1=b, op=op), reads, writes, tag=tag)

    def ts(self, out, a, s1, s2, op0, op1, reads, writes, eng="vector", tag=""):
        if s2 is None:
            return self.sch.op(eng, lambda e: e.tensor_scalar(out=out, in0=a, scalar1=s1, scalar2=None, op0=op0), reads, writes, tag=tag)
        return self.sch.op(eng, lambda e: e.tensor_scalar(out=out, in0=a, scalar1=s1, scalar2=s2, op0=op0, op1=op1), reads, writes, tag=tag)

    def stt(self, out, a, s, b, op0, op1, reads, writes, eng="vector", tag=""):
        return self.sch.op(eng, lambda e: e.scalar_tensor_tensor(out=out, in0=a, scalar=s, in1=b, op0=op0, op1=op1), reads, writes, tag=tag)

    def copy(self, out, a, reads, writes, eng="vector", tag=""):
        return self.sch.op(eng, lambda e: e.tensor_copy(out=out, in_=a), reads, writes, tag=tag)

    def act(self, out, a, func, bias=0.0, scale=1.0, reads=(), writes=(), tag=""):
        return self.sch.op("scalar", lambda e: e.activation(out=out, in_=a, func=func, bias=bias, scale=scale), reads, writes, tag=tag)

    def recip(self, out, a, reads, writes, tag=""):
        return self.sch.op("vector", lambda e: e.reciprocal(out=out, in_=a), reads, writes, tag=tag)

    def reduce(self, out, a, op, reads, writes, tag=""):
        return self.sch.op("vector", lambda e: e.tensor_reduce(out=out, in_=a, axis=AXX, op=op), reads, writes, tag=tag)

    def scan(self, out, d0, d1, init, op0, op1, reads, writes, tag=""):
        return self.sch.op("vector", lambda e: e.tensor_tensor_scan(out=out, data0=d0, data1=d1, initial=init, op0=op0, op1=op1), reads, writes, tag=tag)

    def mmgroup(self, mms, reads, writes, tag=""):
        def f(e):
            ins = None
            n = len(mms)
            for i, m in enumerate(mms):
                if len(m) == 3:
                    o, l, r = m
                    st, sp = (i == 0), (i == n - 1)
                else:
                    o, l, r, st, sp = m
                ins = e.matmul(out=o, lhsT=l, rhs=r, start=st, stop=sp)
            return ins
        return self.sch.op("tensor", f, reads, writes, tag=tag)


CC = {}
_cc_vals = []


def _cc(name, val):
    CC[name] = len(_cc_vals)
    _cc_vals.append(np.float32(val))


_cc("zero", 0.0)
_cc("one", 1.0)
_cc("eps35", 2.0 ** -35)
_cc("tiny", 1e-38)
_cc("ln_eps", 1e-5)
_cc("eps6", 1e-6)
_cc("b8ln2", 8.0 * LN2)
_cc("half", 0.5)
NCC = len(_cc_vals)

DC = {n: i for i, n in enumerate(
    ["flag", "bcol", "mbcol", "bs5", "bs4", "thr_fc", "mthr_fc", "thr_pj", "mthr_pj"])}
NDC = 12


def host_consts():
    cc = np.zeros((128, NCC), np.float32)
    for i, v in enumerate(_cc_vals):
        cc[:, i] = v
    return cc


def build_program(phases, debug_out=()):
    nc = bass.Bass("TRN2", target_bir_lowering=False, debug=False)
    stack = contextlib.ExitStack()
    sch = Sched(nc, stack)
    k = K(nc, stack, sch)
    dbg = set(debug_out)

    def dr(name, shape, dtype):
        kind = "ExternalOutput" if name in dbg else "Internal"
        return k.dram_t(name, shape, dtype, kind=kind)

    x_in = k.dram_t("x", [NSEQ, NFC, 128, SEQ], F32, kind="ExternalInput")
    cc_in = k.dram_t("cc", [128, NCC], F32, kind="ExternalInput")
    onesrow = k.dram_t("ones_row", [1, 128], F32, kind="ExternalInput")
    one11 = k.dram_t("one_11", [1, 1], F32, kind="ExternalInput")
    masks_in = k.dram_t("masks", [4, 128, 128], F32, kind="ExternalInput")
    bias_in = k.dram_t("biases", [128, 160], F32, kind="ExternalInput")
    dmask_in = k.dram_t("dmask", [NSEQ, NFC, 128, SEQ], BF16, kind="ExternalInput")
    w_l0qkv = k.dram_t("w_l0qkv", [3, D, D], F32, kind="ExternalInput")
    w_l0o = k.dram_t("w_l0o", [D, D], F32, kind="ExternalInput")
    w_fcT = k.dram_t("w_fcT", [D, HFF], F32, kind="ExternalInput")
    w_pjT = k.dram_t("w_pjT", [HFF, D], F32, kind="ExternalInput")
    w_l1p = k.dram_t("w_l1p", [5, D, C1], F32, kind="ExternalInput")
    w_l1o = k.dram_t("w_l1o", [C1, D], F32, kind="ExternalInput")
    w_1T = k.dram_t("w_1T", [D, HFF], F32, kind="ExternalInput")
    w_2T = k.dram_t("w_2T", [HFF, D], F32, kind="ExternalInput")
    xout = k.dram_t("xout", [NSEQ, NFC, 128, SEQ], F32, kind="ExternalOutput")

    x1_d = dr("x1", [NSEQ, NFC, 128, SEQ], F32)
    x2_d = dr("x2", [NSEQ, NFC, 128, SEQ], F32)
    xsp_d = dr("xspl", [NSEQ, 2, NFC, 128, SEQ], BF16)
    wsp = dr("wsp", [2, 4, D, D], BF16)
    wfcq_d = dr("wfcq", [D, HFF], BF16)
    wpjq_d = dr("wpjq", [HFF, D], BF16)
    w1q_d = dr("w1q", [D, HFF], FP8)
    w2q_d = dr("w2q", [HFF, D], FP8)
    wl1sp = dr("wl1sp", [2, 5, D, C1], BF16)
    wl1osp = dr("wl1osp", [2, C1, D], BF16)
    qfm_d = dr("qfm", [NSEQ, C1, SEQ], F32)
    kfm_d = dr("kfm", [NSEQ, C1, SEQ], F32)
    gfm_d = dr("gfm", [NSEQ, C1, SEQ], F32)
    ktm_d = dr("ktm", [NSEQ, 32, 128, C1], F32)
    vtm_d = dr("vtm", [NSEQ, 32, 128, C1], F32)
    rows_d = dr("rows", [NSEQ, 4, SEQ], F32)
    lac_d = dr("lacols", [NSEQ, 32, 128, 4], F32)
    x3_d = dr("x3", [NSEQ, NFC, 128, SEQ], F32)
    ofm_d = dr("ofm", [NSEQ, C1, SEQ], F32)

    ccsb = k.sbuf("ccsb", [128, NCC], F32)
    dcsb = k.sbuf("dcsb", [128, NDC], F32)
    accsb = k.sbuf("accsb", [128, 64], F32)
    ones_sb = k.sbuf("ones1", [1, 128], F32)
    one11_sb = k.sbuf("one11", [1, 1], F32)
    mask_sb = k.sbuf("masksb", [128, 4, 128], F32)
    bias_sb = k.sbuf("biassb", [128, 160], F32)
    WB = k.sbuf("WB", [128, 32 * 1024], BF16)
    WS = k.sbuf("WS", [128, 4, 1024], BF16)
    H = k.sbuf("H", [128, NHC, 512], F32)
    A1 = k.sbuf("A1", [128, NFC, 512], F32)
    A2 = k.sbuf("A2", [128, NFC, 512], F32)
    BH1 = k.sbuf("BH1", [128, NFC, 512], BF16)
    BH2 = k.sbuf("BH2", [128, NFC, 512], BF16)
    BQ8 = k.sbuf("BQ8", [128, 2, NFC, 512], FP8)
    Q8S = BH1[:].bitcast(FP8)
    R1 = k.sbuf("R1", [128, 512], F32)
    R2 = k.sbuf("R2", [128, 512], F32)
    ST = k.sbuf("STr", [1, 1024], F32)
    PROW = k.sbuf("PROW", [2, 2 * 512], F32)
    pb = [k.psum(f"pb{i}") for i in range(8)]
    Hf = H[:].rearrange("p a b -> p (a b)")

    def ccol(name):
        return ccsb[:, CC[name]:CC[name] + 1]

    def dcol(name):
        return dcsb[:, DC[name]:DC[name] + 1]

    def flat(t, n):
        return t[:].rearrange("p a b -> p (a b)")[:, 0:n]

    k.dma("sync", ccsb[:], cc_in.ap(), [], ["ccsb"], tag="ld_cc")
    k.dma("sync", ones_sb[:], onesrow.ap(), [], ["ones1"], tag="ld_ones")
    k.dma("sync", one11_sb[:], one11.ap(), [], ["one11"], tag="ld_one11")
    k.dma("sync", mask_sb[:], masks_in.ap().transpose([1, 0, 2]), [], ["masksb"], tag="ld_masks")
    k.dma("sync", bias_sb[:], bias_in.ap(), [], ["biassb"], tag="ld_bias")

    k.ts(dcol("flag"), ccol("eps35"), 1.0, 1.0, ALU.add, ALU.subtract, ["ccsb"], ["dcsb"], tag="flag0")
    k.ts(dcol("flag"), dcol("flag"), 0.0, None, ALU.is_gt, None, ["dcsb"], ["dcsb"], tag="flag1")
    k.ts(dcol("bcol"), dcol("flag"), B64, B32, ALU.mult, ALU.max, ["dcsb"], ["dcsb"], tag="bcol")
    k.ts(dcol("mbcol"), dcol("flag"), -B64, -B32, ALU.mult, ALU.min, ["dcsb"], ["dcsb"], tag="mbcol")
    for nm, M in (("bs5", 10), ("bs4", 3)):
        base = float(math.log(1.5) + 23 * LN2 - M * LN2)
        k.ts(dcol(nm), dcol("flag"), 29.0 * LN2, base, ALU.mult, ALU.add, ["dcsb"], ["dcsb"], tag=f"dc_{nm}")

    def wprep_split(src_ap, n_ic, n_out, hi_ap, lo_ap, name):
        for icx in range(n_ic):
            wt = flat(H, n_out)
            hi = flat(BH1, n_out)
            lo = flat(BH2, n_out)
            k.dma("sync", wt, src_ap[icx * 128:(icx + 1) * 128, :], ["H"], ["H"], tag=f"{name}_ld{icx}")
            k.copy(hi, wt, ["H"], ["BH1"], tag=f"{name}_hi{icx}")
            k.tt(lo, wt, hi, ALU.subtract, ["H", "BH1"], ["BH2"], tag=f"{name}_lo{icx}")
            k.dma("gpsimd", hi_ap[icx * 128:(icx + 1) * 128, :], hi, ["BH1"], [f"d_{name}"], tag=f"{name}_sthi{icx}")
            k.dma("gpsimd", lo_ap[icx * 128:(icx + 1) * 128, :], lo, ["BH2"], [f"d_{name}"], tag=f"{name}_stlo{icx}")

    if "p0w" in phases:
        for wi in range(3):
            wprep_split(w_l0qkv.ap()[wi], NFC, D, wsp.ap()[0, wi], wsp.ap()[1, wi], f"l0w{wi}")
        wprep_split(w_l0o.ap(), NFC, D, wsp.ap()[0, 3], wsp.ap()[1, 3], "l0wo")
        for wi in range(5):
            wprep_split(w_l1p.ap()[wi], NFC, C1, wl1sp.ap()[0, wi], wl1sp.ap()[1, wi], f"l1w{wi}")
        wprep_split(w_l1o.ap(), 1, D, wl1osp.ap()[0], wl1osp.ap()[1], "l1wo")

    def emit_ternary(src_ap, n_ic, n_out, thr_name, dst_ap, name):
        for icx in range(n_ic):
            wt = flat(H, n_out)
            ab = flat(A2, n_out)
            k.dma("sync", wt, src_ap[icx * 128:(icx + 1) * 128, :], ["H"], ["H"], tag=f"{name}_gld{icx}")
            k.act(ab, wt, ACT.Abs, reads=["H"], writes=["A2"], tag=f"{name}_abs{icx}")
            k.reduce(accsb[:, 32 + icx:33 + icx], ab, ALU.add, ["A2"], ["accsb"], tag=f"{name}_red{icx}")
        k.reduce(accsb[:, 32:33], accsb[:, 32:32 + n_ic], ALU.add,
                 ["accsb"], ["accsb"], tag=f"{name}_colred")
        k.mmgroup([(pb[0][0:1, 0:1], accsb[:, 32:33], ccol("one"))], ["accsb", "ccsb"], ["pb0"], tag=f"{name}_tot")
        tot = ST[0:1, 0:1]
        k.ts(tot, pb[0][0:1, 0:1], 1.0 / (n_ic * 128 * n_out), None, ALU.mult, None, ["pb0"], ["STr"], tag=f"{name}_tot1")
        k.ts(tot, tot, 1e-6, 0.5, ALU.add, ALU.mult, ["STr"], ["STr"], tag=f"{name}_thrv")
        k.mmgroup([(pb[0][:, 1:2], ones_sb[:], tot)], ["STr", "ones1"], ["pb0"], tag=f"{name}_bc")
        k.copy(dcol(thr_name), pb[0][:, 1:2], ["pb0"], ["dcsb"], tag=f"{name}_thr")
        k.ts(dcol("m" + thr_name), pb[0][:, 1:2], -1.0, None, ALU.mult, None, ["pb0"], ["dcsb"], tag=f"{name}_mthr")
        for icx in range(n_ic):
            wt = flat(H, n_out)
            p_ = flat(A2, n_out)
            n_ = Hf[:, 8192:8192 + n_out]
            qv = flat(BH1, n_out)
            k.dma("sync", wt, src_ap[icx * 128:(icx + 1) * 128, :], ["H"], ["H"], tag=f"{name}_qld{icx}")
            k.ts(p_, wt, dcol(thr_name), None, ALU.is_ge, None, ["H", "dcsb"], ["A2"], tag=f"{name}_p{icx}")
            k.ts(n_, wt, dcol("m" + thr_name), None, ALU.is_le, None, ["H", "dcsb"], ["H"], tag=f"{name}_n{icx}")
            k.tt(qv, p_, n_, ALU.subtract, ["A2", "H"], ["BH1"], tag=f"{name}_q{icx}")
            k.dma("gpsimd", dst_ap[icx * 128:(icx + 1) * 128, :], qv, ["BH1"], [f"d_{name}"], tag=f"{name}_qst{icx}")

    if "p0t" in phases:
        emit_ternary(w_fcT.ap(), NFC, HFF, "thr_fc", wfcq_d.ap(), "tfc")
        emit_ternary(w_pjT.ap(), NHC, D, "thr_pj", wpjq_d.ap(), "tpj")

    def emit_q4exact(src, n, dst, reads, writes, name, relu=False, t1b=None, t2b=None,
                     r1="A2", r2="H"):
        t1 = t1b if t1b is not None else flat(A2, n)
        t2 = t2b if t2b is not None else Hf[:, 8192:8192 + n]
        r_ = list(reads)
        k.act(t1, src, ACT.Abs, reads=r_, writes=[r1], tag=f"{name}_abs")
        k.act(t1, t1, ACT.Ln, bias=ccol("tiny"), reads=[r1, "ccsb"], writes=[r1], tag=f"{name}_ln")
        k.ts(t1, t1, 1.0 / LN2, -0.5, ALU.mult, ALU.add, [r1], [r1], tag=f"{name}_v")
        k.ts(t1, t1, dcol("bcol"), dcol("mbcol"), ALU.add, ALU.add, [r1, "dcsb"], [r1], tag=f"{name}_e0")
        k.ts(t1, t1, -6.0, 8.0, ALU.max, ALU.min, [r1], [r1], tag=f"{name}_e4")
        k.act(t1, t1, ACT.Exp, bias=ccol("b8ln2"), scale=-LN2, reads=[r1, "ccsb"], writes=[r1], tag=f"{name}_p")
        k.ts(t1, t1, dcol("bcol"), dcol("mbcol"), ALU.add, ALU.add, [r1, "dcsb"], [r1], tag=f"{name}_pe")
        k.stt(t2, src, 2.0 ** -5, t1, ALU.mult, ALU.mult, r_ + [r1], [r2], tag=f"{name}_t")
        k.ts(t2, t2, dcol("bcol"), dcol("mbcol"), ALU.add, ALU.add, [r2, "dcsb"], [r2], tag=f"{name}_n")
        k.act(t1, t1, ACT.Ln, reads=[r1], writes=[r1], tag=f"{name}_rl")
        k.ts(t1, t1, -1.0, 14.0 * LN2, ALU.mult, ALU.add, [r1], [r1], tag=f"{name}_rn")
        k.act(t1, t1, ACT.Exp, reads=[r1], writes=[r1], tag=f"{name}_rx")
        k.ts(t1, t1, dcol("bcol"), dcol("mbcol"), ALU.add, ALU.add, [r1, "dcsb"], [r1], tag=f"{name}_se")
        k.stt(dst, t2, 2.0 ** -9, t1, ALU.mult, ALU.mult, [r1, r2], list(writes), tag=f"{name}_q")

    def emit_wq4(src_ap, n_ic, n_out, dst_ap, name):
        for icx in range(n_ic):
            wt = flat(H, n_out)
            qv = flat(BH1, n_out)
            k.dma("sync", wt, src_ap[icx * 128:(icx + 1) * 128, :], ["H"], ["H"], tag=f"{name}_ld{icx}")
            emit_q4exact(wt, n_out, qv, ["H", "dcsb", "ccsb"], ["BH1"], f"{name}{icx}")
            if name == "qw1" and icx == 0 and "q4dbg" in dbg:
                q4dump = k.dram_t("q4dump", [3, 128, 512], F32, kind="ExternalOutput")
                k.dma("gpsimd", q4dump.ap()[0], flat(A2, 512), ["A2"], [], tag="q4d_t1")
                k.dma("gpsimd", q4dump.ap()[1], Hf[:, 8192:8192 + 512], ["H"], [], tag="q4d_t2")
                dn = k.dma("gpsimd", q4dump.ap()[2], wt[:, 0:512], ["H"], [], tag="q4d_wt")
                sch.barrier_end(dn)
            k.dma("gpsimd", dst_ap[icx * 128:(icx + 1) * 128, :], qv, ["BH1"], [f"d_{name}"], tag=f"{name}_st{icx}")

    if "p0q" in phases:
        emit_wq4(w_1T.ap(), NFC, HFF, w1q_d.ap(), "qw1")
        emit_wq4(w_2T.ap(), NHC, D, w2q_d.ap(), "qw2")
        emit_q4exact(bias_sb[:, 32:64], 32, bias_sb[:, 32:64], ["biassb", "dcsb", "ccsb"], ["biassb"], "qb1")
        emit_q4exact(bias_sb[:, 64:72], 8, bias_sb[:, 64:72], ["biassb", "dcsb", "ccsb"], ["biassb"], "qb2")

    def emit_ln(xap, n_fc, T, deps, name, sq, sqres, hi=None, lo=None, hires=None, lores=None,
               xn=None, xnres=None, psA=0, psB=1, par=0):
        st_t = ST if par == 0 else PROW[0:1, :]
        str_ = "STr" if par == 0 else "PROW0"
        mu2 = (R1 if par == 0 else R2)[0:1, 0:T]
        m2t = "R1" if par == 0 else "R2"
        k.act(sq, xap, ACT.Square, reads=deps, writes=[sqres], tag=f"{name}_sq")
        k.mmgroup([(pb[psA][0:1, 0:T], ccol("one"), xap[:, fc, :]) for fc in range(n_fc)],
                  deps + ["ccsb"], [f"pb{psA}"], tag=f"{name}_must")
        k.mmgroup([(pb[psB][0:1, 0:T], ccol("one"), sq[:, fc, :]) for fc in range(n_fc)],
                  [sqres, "ccsb"], [f"pb{psB}"], tag=f"{name}_sqst")
        mu = st_t[0:1, 0:T]
        sqr = st_t[0:1, 512:512 + T]
        var = sqr
        rr = sqr
        mur = mu
        nfeat = n_fc * 128
        k.ts(mu, pb[psA][0:1, 0:T], 1.0 / nfeat, None, ALU.mult, None, [f"pb{psA}"], [str_], tag=f"{name}_mu")
        k.ts(sqr, pb[psB][0:1, 0:T], 1.0 / nfeat, None, ALU.mult, None, [f"pb{psB}", str_], [str_], tag=f"{name}_sqm")
        k.tt(mu2, mu, mu, ALU.mult, [str_], [m2t], tag=f"{name}_mu2")
        k.tt(var, sqr, mu2, ALU.subtract, [str_, m2t], [str_], tag=f"{name}_var")
        k.act(var, var, ACT.Ln, bias=ccsb[0:1, CC["ln_eps"]:CC["ln_eps"] + 1],
              reads=[str_, "ccsb"], writes=[str_], tag=f"{name}_lnv")
        k.ts(var, var, -0.5, None, ALU.mult, None, [str_], [str_], tag=f"{name}_hf")
        k.act(rr, var, ACT.Exp, reads=[str_], writes=[str_], tag=f"{name}_r")
        k.tt(mur, mu, rr, ALU.mult, [str_], [str_], tag=f"{name}_mur")
        k.mmgroup([(pb[psA][:, 0:T], ones_sb[:], rr)], [str_, "ones1"], [f"pb{psA}"], tag=f"{name}_rbc")
        k.mmgroup([(pb[psB][:, 0:T], ones_sb[:], mur)], [str_, "ones1"], [f"pb{psB}"], tag=f"{name}_murbc")
        rb = pb[psA][:, 0:T].unsqueeze(1).broadcast_to([128, n_fc, T])
        murb = pb[psB][:, 0:T].unsqueeze(1).broadcast_to([128, n_fc, T])
        k.tt(sq, xap, rb, ALU.mult, deps + [f"pb{psA}"], [sqres], tag=f"{name}_t4")
        if xn is not None:
            k.tt(xn, sq, murb, ALU.subtract, [sqres, f"pb{psB}"], [xnres], tag=f"{name}_xn")
            if hi is not None:
                k.copy(hi, xn, [xnres], [hires], tag=f"{name}_hi")
                k.tt(lo, xn, hi, ALU.subtract, [xnres, hires], [lores], tag=f"{name}_lo")
        else:
            k.tt(hi, sq, murb, ALU.subtract, [sqres, f"pb{psB}"], [hires], tag=f"{name}_hi")
            k.tt(sq, sq, murb, ALU.subtract, [sqres, f"pb{psB}"], [sqres], tag=f"{name}_xnf")
            k.tt(lo, sq, hi, ALU.subtract, [sqres, hires], [lores], tag=f"{name}_lo")

    def wb(off_kb, n):
        start = off_kb * 512
        return WB[:, start:start + n]

    def wmat(idx):
        return wb(16 * idx, NFC * D).rearrange("p (a b) -> p a b", a=NFC)

    def load_wmat(idx, src_ap, tag, dname):
        return k.dma("sync", wmat(idx), src_ap.rearrange("(a p) b -> p a b", p=128),
                     [f"WB{idx}", dname], [f"WB{idx}"], tag=tag)

    T1 = 512
    NT1 = SEQ // T1
    KVACC = accsb[:, 0:2 * NFC]
    KSACC = accsb[:, 2 * NFC:4 * NFC]
    sch.fence_from(["H"], ["Hq0", "Hq1", "Hq2", "Hq3"])

    if "p1" in phases:
        load_wmat(0, wsp.ap()[0, 1], "ld_wkh", "d_l0w1")
        load_wmat(1, wsp.ap()[1, 1], "ld_wkl", "d_l0w1")
        load_wmat(2, wsp.ap()[0, 2], "ld_wvh", "d_l0w2")
        load_wmat(3, wsp.ap()[1, 2], "ld_wvl", "d_l0w2")
        k.ts(accsb[:, 0:4 * NFC], accsb[:, 0:4 * NFC], 0.0, None, ALU.mult, None, ["accsb"], ["accsb"], tag="p1_zero")
        for s in range(NSEQ):
            for it in range(NT1):
                t0 = it * T1
                xt = A1[:, :, 0:T1]
                k.dma("sync", xt, x_in.ap()[s, :, :, t0:t0 + T1].transpose([1, 0, 2]), ["A1"], ["A1"], tag=f"p1_ldx{s}_{it}")
                emit_ln(xt, NFC, T1, ["A1"], f"p1ln{s}_{it}", A2[:, :, 0:T1], "A2",
                        hi=BH1[:, :, 0:T1], lo=BH2[:, :, 0:T1], hires="BH1", lores="BH2")
                k.dma("scalar", xsp_d.ap()[s, 0, :, :, t0:t0 + T1].transpose([1, 0, 2]), BH1[:, :, 0:T1],
                      ["BH1"], [f"xsp0:{s}:{it}"], tag=f"p1_sh{s}_{it}")
                k.dma("scalar", xsp_d.ap()[s, 1, :, :, t0:t0 + T1].transpose([1, 0, 2]), BH2[:, :, 0:T1],
                      ["BH2"], [f"xsp1:{s}:{it}"], tag=f"p1_sl{s}_{it}")
                if s == 0 and it == 0 and "p1dbg" in dbg:
                    p1dump = k.dram_t("p1dump", [4, 128, 512], F32, kind="ExternalOutput")
                    p1dumpb = k.dram_t("p1dumpb", [2, 128, NFC, 512], BF16, kind="ExternalOutput")
                    k.dma("gpsimd", p1dump.ap()[0], A1[:, 0, 0:T1], ["A1"], [], tag="p1d_x")
                    k.dma("gpsimd", p1dump.ap()[1], R1[:, 0:T1], ["R1"], [], tag="p1d_r")
                    k.dma("gpsimd", p1dump.ap()[2], R2[:, 0:T1], ["R2"], [], tag="p1d_mur")
                    k.dma("gpsimd", p1dumpb.ap()[0], BH1[:, :, 0:T1], ["BH1"], [], tag="p1d_hi")
                    dn = k.dma("gpsimd", p1dumpb.ap()[1], BH2[:, :, 0:T1], ["BH2"], [], tag="p1d_lo")
                    sch.barrier_end(dn)
                for pi, (whi, wlo, bcol0, wres) in enumerate([(0, 1, 8, "WB0"), (2, 3, 16, "WB2")]):
                    isk = pi == 0
                    for oc in range(NFC):
                        bank = 2 + (pi * NFC + oc) % 4
                        mms = []
                        for kc in range(NFC):
                            lh = wmat(whi)[:, kc, oc * 128:(oc + 1) * 128]
                            ll = wmat(whi + 1)[:, kc, oc * 128:(oc + 1) * 128]
                            mms.append((pb[bank][:, 0:T1], lh, BH1[:, kc, 0:T1]))
                            mms.append((pb[bank][:, 0:T1], lh, BH2[:, kc, 0:T1]))
                            mms.append((pb[bank][:, 0:T1], ll, BH1[:, kc, 0:T1]))
                        k.mmgroup(mms, ["BH1", "BH2", f"WB{whi}", f"WB{whi + 1}"], [f"pb{bank}"],
                                  tag=f"p1_mm{s}_{it}_{pi}_{oc}")
                        dstb = H[:, (8 + oc) if isk else oc, 0:T1]
                        k.act(dstb, pb[bank][:, 0:T1], ACT.Identity, bias=bias_sb[:, bcol0 + oc:bcol0 + oc + 1],
                              reads=[f"pb{bank}", "biassb"], writes=["Hq1" if isk else "Hq0"],
                              tag=f"p1_ev{s}_{it}_{pi}_{oc}")
                kk_ = H[:, 8:16, 0:T1]
                mm_ = A2[:, :, 0:T1]
                k.ts(mm_, kk_, 0.0, None, ALU.min, None, ["Hq1"], ["A2"], tag=f"p1_min{s}_{it}")
                k.act(mm_, mm_, ACT.Exp, reads=["A2"], writes=["A2"], tag=f"p1_exp{s}_{it}")
                k.stt(kk_, kk_, 1.0, mm_, ALU.add, ALU.max, ["Hq1", "A2"], ["Hq1"], tag=f"p1_fm{s}_{it}")
                kv = A2[:, :, 0:T1]
                k.tt(kv, kk_, H[:, 0:NFC, 0:T1], ALU.mult, ["Hq0", "Hq1"], ["A2"], tag=f"p1_kv{s}_{it}")
                pc1 = accsb[:, 40:40 + NFC].unsqueeze(2)
                pc2 = accsb[:, 48:48 + NFC].unsqueeze(2)
                k.reduce(pc1, kv, ALU.add, ["A2"], ["accsb"], tag=f"p1_kvr{s}_{it}")
                k.reduce(pc2, kk_, ALU.add, ["Hq1"], ["accsb"], tag=f"p1_ksr{s}_{it}")
                k.tt(KVACC[:, s * NFC:(s + 1) * NFC], KVACC[:, s * NFC:(s + 1) * NFC],
                     accsb[:, 40:40 + NFC], ALU.add, ["accsb"], ["accsb"], tag=f"p1_kva{s}_{it}")
                k.tt(KSACC[:, s * NFC:(s + 1) * NFC], KSACC[:, s * NFC:(s + 1) * NFC],
                     accsb[:, 48:48 + NFC], ALU.add, ["accsb"], ["accsb"], tag=f"p1_ksa{s}_{it}")

    if "p2a" in phases:
        load_wmat(0, wsp.ap()[0, 0], "ld_wqh", "d_l0w0")
        load_wmat(1, wsp.ap()[1, 0], "ld_wql", "d_l0w0")
        load_wmat(2, wsp.ap()[0, 3], "ld_woh", "d_l0wo")
        load_wmat(3, wsp.ap()[1, 3], "ld_wol", "d_l0wo")
        for s in range(NSEQ):
            kvb = KVACC[:, s * NFC:(s + 1) * NFC].unsqueeze(2).broadcast_to([128, NFC, T1])
            ksb = KSACC[:, s * NFC:(s + 1) * NFC].unsqueeze(2).broadcast_to([128, NFC, T1])
            for it in range(NT1):
                t0 = it * T1
                xdn = f"x1:{s}:{it}"
                k.dma("scalar", x1_d.ap()[s, :, :, t0:t0 + T1], x_in.ap()[s, :, :, t0:t0 + T1],
                      [xdn], [xdn], tag=f"p2a_pc{s}_{it}")
                k.dma("sync", BH1[:, :, 0:T1], xsp_d.ap()[s, 0, :, :, t0:t0 + T1].transpose([1, 0, 2]),
                      ["BH1", f"xsp0:{s}:{it}"], ["BH1"], tag=f"p2a_lh{s}_{it}")
                k.dma("sync", BH2[:, :, 0:T1], xsp_d.ap()[s, 1, :, :, t0:t0 + T1].transpose([1, 0, 2]),
                      ["BH2", f"xsp1:{s}:{it}"], ["BH2"], tag=f"p2a_ll{s}_{it}")
                for oc in range(NFC):
                    bank = 2 + oc % 4
                    mms = []
                    for kc in range(NFC):
                        lh = wmat(0)[:, kc, oc * 128:(oc + 1) * 128]
                        ll = wmat(1)[:, kc, oc * 128:(oc + 1) * 128]
                        mms.append((pb[bank][:, 0:T1], lh, BH1[:, kc, 0:T1]))
                        mms.append((pb[bank][:, 0:T1], lh, BH2[:, kc, 0:T1]))
                        mms.append((pb[bank][:, 0:T1], ll, BH1[:, kc, 0:T1]))
                    k.mmgroup(mms, ["BH1", "BH2", "WB0", "WB1"], [f"pb{bank}"], tag=f"p2a_qmm{s}_{it}_{oc}")
                    k.act(H[:, oc, 0:T1], pb[bank][:, 0:T1], ACT.Identity, bias=bias_sb[:, oc:oc + 1],
                          reads=[f"pb{bank}", "biassb"], writes=["Hq0"], tag=f"p2a_qev{s}_{it}_{oc}")
                qq = H[:, 0:NFC, 0:T1]
                mm_ = A2[:, :, 0:T1]
                k.ts(mm_, qq, 0.0, None, ALU.min, None, ["Hq0"], ["A2"], tag=f"p2a_min{s}_{it}")
                k.act(mm_, mm_, ACT.Exp, reads=["A2"], writes=["A2"], tag=f"p2a_exp{s}_{it}")
                k.stt(qq, qq, 1.0, mm_, ALU.add, ALU.max, ["Hq0", "A2"], ["Hq0"], tag=f"p2a_fm{s}_{it}")
                u = A2[:, :, 0:T1]
                k.tt(u, qq, ksb, ALU.mult, ["Hq0", "accsb"], ["A2"], tag=f"p2a_u{s}_{it}")
                k.act(u, u, ACT.Ln, bias=ccol("eps6"), reads=["A2", "ccsb"], writes=["A2"], tag=f"p2a_ul{s}_{it}")
                k.ts(u, u, -1.0, None, ALU.mult, None, ["A2"], ["A2"], tag=f"p2a_un{s}_{it}")
                k.act(u, u, ACT.Exp, reads=["A2"], writes=["A2"], tag=f"p2a_rc{s}_{it}")
                at = H[:, 8:16, 0:T1]
                k.tt(at, qq, u, ALU.mult, ["Hq0", "A2"], ["Hq1"], tag=f"p2a_a1{s}_{it}")
                k.tt(at, at, kvb, ALU.mult, ["Hq1", "accsb"], ["Hq1"], tag=f"p2a_a2{s}_{it}")
                k.copy(BH1[:, :, 0:T1], at, ["Hq1"], ["BH1"], tag=f"p2a_hi{s}_{it}")
                k.tt(BH2[:, :, 0:T1], at, BH1[:, :, 0:T1], ALU.subtract, ["Hq1", "BH1"], ["BH2"], tag=f"p2a_lo{s}_{it}")
                for oc in range(NFC):
                    bank = 2 + oc % 4
                    mms = []
                    for kc in range(NFC):
                        lh = wmat(2)[:, kc, oc * 128:(oc + 1) * 128]
                        ll = wmat(3)[:, kc, oc * 128:(oc + 1) * 128]
                        mms.append((pb[bank][:, 0:T1], lh, BH1[:, kc, 0:T1]))
                        mms.append((pb[bank][:, 0:T1], lh, BH2[:, kc, 0:T1]))
                        mms.append((pb[bank][:, 0:T1], ll, BH1[:, kc, 0:T1]))
                    k.mmgroup(mms, ["BH1", "BH2", "WB2", "WB3"], [f"pb{bank}"], tag=f"p2a_omm{s}_{it}_{oc}")
                    k.act(H[:, 16 + oc, 0:T1], pb[bank][:, 0:T1], ACT.Gelu_apprx_tanh,
                          bias=bias_sb[:, 24 + oc:24 + oc + 1],
                          reads=[f"pb{bank}", "biassb"], writes=["Hq2"], tag=f"p2a_oev{s}_{it}_{oc}")
                emit_ln(H[:, 16:24, 0:T1], NFC, T1, ["Hq2"], f"p2aln2{s}_{it}", A2[:, :, 0:T1], "A2",
                        xn=H[:, 24:32, 0:T1], xnres="Hq3")
                k.sch.op("gpsimd", lambda e, s=s, t0=t0: e.dma_start(
                    out=x1_d.ap()[s, :, :, t0:t0 + T1].transpose([1, 0, 2]),
                    in_=H[:, 24:32, 0:T1], accum_op=ALU.add),
                    ["Hq3", xdn], [xdn], dma=True, tag=f"p2a_st{s}_{it}")

    if "p2b" in phases:
        WBv = WB[:, 0:NFC * HFF].rearrange("p (a b) -> p a b", a=NFC)
        k.dma("sync", WBv, wfcq_d.ap().rearrange("(a p) b -> p a b", p=128),
              ["WB0", "WB1", "WB2", "WB3", "d_tfc"], ["WB0", "WB1", "WB2", "WB3"], tag="p2b_ldwfc")
        WSv = WS[:, :, 0:1024]
        for s in range(NSEQ):
            for it in range(NT1):
                t0 = it * T1
                xdn = f"x2:{s}:{it}"
                x1n = f"x1:{s}:{it}"
                xt = A1[:, :, 0:T1]
                k.dma("sync", xt, x1_d.ap()[s, :, :, t0:t0 + T1].transpose([1, 0, 2]), ["A1", x1n], ["A1"], tag=f"p2b_ldx{s}_{it}")
                k.dma("scalar", x2_d.ap()[s, :, :, t0:t0 + T1], x1_d.ap()[s, :, :, t0:t0 + T1],
                      [xdn, x1n], [xdn], tag=f"p2b_pc{s}_{it}")
                emit_ln(xt, NFC, T1, ["A1"], f"p2bln{s}_{it}", A2[:, :, 0:T1], "A2",
                        hi=BH1[:, :, 0:T1], lo=BH2[:, :, 0:T1], hires="BH1", lores="BH2")
                for hc in range(NHC):
                    bank = 2 + hc % 2
                    mms = []
                    for kc in range(NFC):
                        wq_ = WBv[:, kc, hc * 128:(hc + 1) * 128]
                        mms.append((pb[bank][:, 0:T1], wq_, BH1[:, kc, 0:T1]))
                        mms.append((pb[bank][:, 0:T1], wq_, BH2[:, kc, 0:T1]))
                    k.mmgroup(mms, ["BH1", "BH2", "WB0"], [f"pb{bank}"], tag=f"p2b_hmm{s}_{it}_{hc}")
                    k.act(H[:, hc, 0:T1], pb[bank][:, 0:T1], ACT.Gelu,
                          reads=[f"pb{bank}"], writes=[f"Hq{hc // 8}"], tag=f"p2b_hev{s}_{it}_{hc}")
                HQALL = ["Hq0", "Hq1", "Hq2", "Hq3"]
                k.mmgroup([(pb[0][0:1, 0:T1], ccol("one"), H[:, hc, 0:T1]) for hc in range(NHC)],
                          HQALL + ["ccsb"], ["pb0"], tag=f"p2b_must{s}_{it}")
                for g in range(4):
                    k.act(A2[:, :, 0:T1], H[:, g * NFC:(g + 1) * NFC, 0:T1], ACT.Square,
                          reads=[f"Hq{g}"], writes=["A2"], tag=f"p2b_sq{s}_{it}_{g}")
                    k.mmgroup([(pb[1][0:1, 0:T1], ccol("one"), A2[:, fc, 0:T1],
                                g == 0 and fc == 0, g == 3 and fc == NFC - 1) for fc in range(NFC)],
                              ["A2", "ccsb"], ["pb1"], tag=f"p2b_sqst{s}_{it}_{g}")
                mu = ST[0:1, 0:T1]
                sqr = ST[0:1, 512:512 + T1]
                rr = sqr
                mur = mu
                k.ts(mu, pb[0][0:1, 0:T1], 1.0 / HFF, None, ALU.mult, None, ["pb0"], ["STr"], tag=f"p2b_mu{s}_{it}")
                k.ts(sqr, pb[1][0:1, 0:T1], 1.0 / HFF, None, ALU.mult, None, ["pb1", "STr"], ["STr"], tag=f"p2b_sqm{s}_{it}")
                mu2b = R1[0:1, 0:T1]
                k.tt(mu2b, mu, mu, ALU.mult, ["STr"], ["R1"], tag=f"p2b_mu2{s}_{it}")
                k.tt(sqr, sqr, mu2b, ALU.subtract, ["STr", "R1"], ["STr"], tag=f"p2b_var{s}_{it}")
                k.act(sqr, sqr, ACT.Ln, bias=ccsb[0:1, CC["ln_eps"]:CC["ln_eps"] + 1],
                      reads=["STr", "ccsb"], writes=["STr"], tag=f"p2b_lnv{s}_{it}")
                k.ts(sqr, sqr, -0.5, None, ALU.mult, None, ["STr"], ["STr"], tag=f"p2b_hf{s}_{it}")
                k.act(rr, sqr, ACT.Exp, reads=["STr"], writes=["STr"], tag=f"p2b_r{s}_{it}")
                k.tt(mur, mu, rr, ALU.mult, ["STr"], ["STr"], tag=f"p2b_mur{s}_{it}")
                k.mmgroup([(pb[0][:, 0:T1], ones_sb[:], rr)], ["STr", "ones1"], ["pb0"], tag=f"p2b_rbc{s}_{it}")
                k.mmgroup([(pb[1][:, 0:T1], ones_sb[:], mur)], ["STr", "ones1"], ["pb1"], tag=f"p2b_murbc{s}_{it}")
                k.copy(R1[:, 0:T1], pb[0][:, 0:T1], ["pb0"], ["R1"], tag=f"p2b_rbev{s}_{it}")
                k.copy(R2[:, 0:T1], pb[1][:, 0:T1], ["pb1"], ["R2"], tag=f"p2b_murbev{s}_{it}")
                rb4 = R1[:, 0:T1].unsqueeze(1).broadcast_to([128, 4, T1])
                murb4 = R2[:, 0:T1].unsqueeze(1).broadcast_to([128, 4, T1])
                for gg in range(8):
                    par = gg % 2
                    csl = slice(gg * 4, (gg + 1) * 4)
                    bsl = slice(par * 4, par * 4 + 4)
                    hq = f"Hq{gg // 2}"
                    bp1 = f"BH1p{par}"
                    bp2 = f"BH2p{par}"
                    Hg = H[:, csl, 0:T1]
                    SCR = A2 if par == 0 else A1
                    rS = "A2" if par == 0 else "A1"
                    lnwb = bias_sb[:, 88 + gg * 4:88 + (gg + 1) * 4].unsqueeze(2).broadcast_to([128, 4, T1])
                    lnbb = bias_sb[:, 120 + gg * 4:120 + (gg + 1) * 4].unsqueeze(2).broadcast_to([128, 4, T1])
                    t_ = SCR[:, 0:4, 0:T1]
                    k.tt(t_, Hg, rb4, ALU.mult, [hq, "R1"], [rS], tag=f"p2b_n1{s}_{it}_{gg}")
                    k.tt(t_, t_, murb4, ALU.subtract, [rS, "R2"], [rS], tag=f"p2b_n2{s}_{it}_{gg}")
                    k.tt(t_, t_, lnwb, ALU.mult, [rS, "biassb"], [rS], tag=f"p2b_n3{s}_{it}_{gg}")
                    k.tt(t_, t_, lnbb, ALU.add, [rS, "biassb"], [rS], tag=f"p2b_n4{s}_{it}_{gg}")
                    k.copy(BH1[:, bsl, 0:T1], t_, [rS], [bp1], tag=f"p2b_shi{s}_{it}_{gg}")
                    k.tt(BH2[:, bsl, 0:T1], t_, BH1[:, bsl, 0:T1], ALU.subtract, [rS, bp1], [bp2], tag=f"p2b_slo{s}_{it}_{gg}")
                    for fc in range(4):
                        hc = gg * 4 + fc
                        k.dma("sync", WS[:, hc % 4, 0:1024], wpjq_d.ap()[hc * 128:(hc + 1) * 128, :],
                              [f"WSa{hc % 4}", "d_tpj"], [f"WSa{hc % 4}"], tag=f"p2b_ldwp{s}_{it}_{hc}")
                    for fc in range(4):
                        hc = gg * 4 + fc
                        mms = []
                        for dc in range(NFC):
                            wp = WS[:, hc % 4, dc * 128:(dc + 1) * 128]
                            st_ = (hc == 0)
                            sp_ = (hc == NHC - 1)
                            mms.append((pb[dc][:, 0:T1], wp, BH1[:, par * 4 + fc, 0:T1], st_, False))
                            mms.append((pb[dc][:, 0:T1], wp, BH2[:, par * 4 + fc, 0:T1], False, sp_))
                        k.mmgroup(mms, [bp1, bp2, f"WSa{hc % 4}"],
                                  [f"pb{dc2}" for dc2 in range(NFC)], tag=f"p2b_ymm{s}_{it}_{hc}")
                for dc in range(NFC):
                    k.act(A2[:, dc, 0:T1], pb[dc][:, 0:T1], ACT.Identity,
                          reads=[f"pb{dc}"], writes=["A2"], tag=f"p2b_yev{s}_{it}_{dc}")
                emit_ln(A2[:, :, 0:T1], NFC, T1, ["A2"], f"p2bln2{s}_{it}", H[:, 8:16, 0:T1], "Hq1",
                        xn=H[:, 0:NFC, 0:T1], xnres="Hq0")
                k.sch.op("gpsimd", lambda e, s=s, t0=t0: e.dma_start(
                    out=x2_d.ap()[s, :, :, t0:t0 + T1].transpose([1, 0, 2]),
                    in_=H[:, 0:NFC, 0:T1], accum_op=ALU.add),
                    ["Hq0", xdn], [xdn], dma=True, tag=f"p2b_st{s}_{it}")

    if "p3" in phases:
        def wl1(widx, part):
            off = (widx * 2 + part) * 1024
            return WB[:, off:off + NFC * C1].rearrange("p (a b) -> p a b", a=NFC)

        for widx in range(5):
            for part in range(2):
                k.dma("sync", wl1(widx, part), wl1sp.ap()[part, widx].rearrange("(a p) b -> p a b", p=128),
                      ["WB0", f"d_l1w{widx}"], ["WB0"], tag=f"p3_ldw{widx}_{part}")
        AB = k.sbuf("ABsb", [128, 16], F32)
        for s in range(NSEQ):
            for it in range(NT1):
                t0 = it * T1
                xt = A1[:, :, 0:T1]
                k.dma("sync", xt, x2_d.ap()[s, :, :, t0:t0 + T1].transpose([1, 0, 2]), ["A1", f"x2:{s}:{it}"], ["A1"], tag=f"p3_ldx{s}_{it}")
                emit_ln(xt, NFC, T1, ["A1"], f"p3ln{s}_{it}", A2[:, :, 0:T1], "A2",
                        hi=BH1[:, :, 0:T1], lo=BH2[:, :, 0:T1], hires="BH1", lores="BH2")
                for pi, (widx, bcol, dstd, sig) in enumerate([(0, 72, qfm_d, False), (1, 73, None, False), (3, 75, gfm_d, True)]):
                    bank = 2 + pi
                    mms = []
                    for kc in range(NFC):
                        lh = wl1(widx, 0)[:, kc, :]
                        ll = wl1(widx, 1)[:, kc, :]
                        mms.append((pb[bank][:, 0:T1], lh, BH1[:, kc, 0:T1]))
                        mms.append((pb[bank][:, 0:T1], lh, BH2[:, kc, 0:T1]))
                        mms.append((pb[bank][:, 0:T1], ll, BH1[:, kc, 0:T1]))
                    k.mmgroup(mms, ["BH1", "BH2", "WB0"], [f"pb{bank}"], tag=f"p3_mm{s}_{it}_{pi}")
                    dst = H[:, pi, 0:T1]
                    k.act(dst, pb[bank][:, 0:T1], ACT.Sigmoid if sig else ACT.Identity,
                          bias=bias_sb[:, bcol:bcol + 1],
                          reads=[f"pb{bank}", "biassb"], writes=["Hq0"], tag=f"p3_ev{s}_{it}_{pi}")
                    if dstd is not None:
                        nmx = "qfm" if pi == 0 else "gfm"
                        k.dma("gpsimd", dstd.ap()[s, :, t0:t0 + T1], dst, ["Hq0"], [f"{nmx}:{s}:{it}"], tag=f"p3_st{s}_{it}_{pi}")
                kraw = H[:, 1, 0:T1]
                sq = A2[:, 0, 0:T1]
                k.act(sq, kraw, ACT.Square, reads=["Hq0"], writes=["A2"], tag=f"p3_ksq{s}_{it}")
                k.mmgroup([(pb[5][0:1, 0:T1], ccol("one"), sq)], ["A2", "ccsb"], ["pb5"], tag=f"p3_kss{s}_{it}")
                nr = ST[0:1, 0:T1]
                k.ts(nr, pb[5][0:1, 0:T1], 1e-24, None, ALU.max, None, ["pb5"], ["STr"], tag=f"p3_kmx{s}_{it}")
                k.act(nr, nr, ACT.Ln, reads=["STr"], writes=["STr"], tag=f"p3_klnv{s}_{it}")
                k.ts(nr, nr, -0.5, None, ALU.mult, None, ["STr"], ["STr"], tag=f"p3_khf{s}_{it}")
                k.act(nr, nr, ACT.Exp, reads=["STr"], writes=["STr"], tag=f"p3_krc{s}_{it}")
                k.mmgroup([(pb[5][:, 0:T1], ones_sb[:], nr)], ["STr", "ones1"], ["pb5"], tag=f"p3_kbc{s}_{it}")
                kn = H[:, 1, 0:T1]
                k.tt(kn, kraw, pb[5][:, 0:T1], ALU.mult, ["Hq0", "pb5"], ["Hq0"], tag=f"p3_knm{s}_{it}")
                k.dma("gpsimd", kfm_d.ap()[s, :, t0:t0 + T1], kn, ["Hq0"], [f"kfm:{s}:{it}"], tag=f"p3_stk{s}_{it}")
                for c4 in range(T1 // 128):
                    cid = it * 4 + c4
                    csl = slice(c4 * 128, (c4 + 1) * 128)
                    mms = []
                    for kc in range(NFC):
                        mms.append((pb[6][:, 0:C1], BH1[:, kc, csl], wl1(2, 0)[:, kc, :]))
                        mms.append((pb[6][:, 0:C1], BH2[:, kc, csl], wl1(2, 0)[:, kc, :]))
                        mms.append((pb[6][:, 0:C1], BH1[:, kc, csl], wl1(2, 1)[:, kc, :]))
                    k.mmgroup(mms, ["BH1", "BH2", "WB0"], ["pb6"], tag=f"p3_vtm{s}_{it}_{c4}")
                    vtm = A2[:, 1, 0:C1]
                    k.act(vtm, pb[6][:, 0:C1], ACT.Identity, bias=bias_sb[:, 74:75],
                          reads=["pb6", "biassb"], writes=["A2"], tag=f"p3_vtev{s}_{it}_{c4}")
                    k.dma("gpsimd", vtm_d.ap()[s, cid], vtm, ["A2"], [f"vtm:{s}:{cid}"], tag=f"p3_stv{s}_{it}_{c4}")
                    mms = []
                    for kc in range(NFC):
                        mms.append((pb[7][:, 0:C1], BH1[:, kc, csl], wl1(1, 0)[:, kc, :]))
                        mms.append((pb[7][:, 0:C1], BH2[:, kc, csl], wl1(1, 0)[:, kc, :]))
                        mms.append((pb[7][:, 0:C1], BH1[:, kc, csl], wl1(1, 1)[:, kc, :]))
                    k.mmgroup(mms, ["BH1", "BH2", "WB0"], ["pb7"], tag=f"p3_ktm{s}_{it}_{c4}")
                    rcol = AB[:, 8:9]
                    k.mmgroup([(pb[4][:, 0:1], ST[0:1, c4 * 128:(c4 + 1) * 128], one11_sb[:])],
                              ["STr", "one11"], ["pb4"], tag=f"p3_rc{s}_{it}_{c4}")
                    k.copy(rcol, pb[4][:, 0:1], ["pb4"], ["ABsb"], tag=f"p3_rcev{s}_{it}_{c4}")
                    ktm = A2[:, 2, 0:C1]
                    k.act(ktm, pb[7][:, 0:C1], ACT.Identity, bias=bias_sb[:, 73:74],
                          reads=["pb7", "biassb"], writes=["A2"], tag=f"p3_ktev{s}_{it}_{c4}")
                    k.ts(ktm, ktm, rcol, None, ALU.mult, None, ["A2", "ABsb"], ["A2"], tag=f"p3_ktn{s}_{it}_{c4}")
                    k.dma("gpsimd", ktm_d.ap()[s, cid], ktm, ["A2"], [f"ktm:{s}:{cid}"], tag=f"p3_stkt{s}_{it}_{c4}")
                    mms = []
                    for kc in range(NFC):
                        mms.append((pb[4][:, 4:6], BH1[:, kc, csl], wl1(4, 0)[:, kc, 0:2]))
                        mms.append((pb[4][:, 4:6], BH2[:, kc, csl], wl1(4, 0)[:, kc, 0:2]))
                        mms.append((pb[4][:, 4:6], BH1[:, kc, csl], wl1(4, 1)[:, kc, 0:2]))
                    k.mmgroup(mms, ["BH1", "BH2", "WB0"], ["pb4"], tag=f"p3_ab{s}_{it}_{c4}")
                    zac = AB[:, 0:1]
                    zbc = AB[:, 6:7]
                    k.act(zac, pb[4][:, 4:5], ACT.Sigmoid, bias=bias_sb[:, 76:77],
                          reads=["pb4", "biassb"], writes=["ABsb"], tag=f"p3_sa{s}_{it}_{c4}")
                    k.act(zbc, pb[4][:, 5:6], ACT.Sigmoid, bias=bias_sb[:, 77:78],
                          reads=["pb4", "biassb"], writes=["ABsb"], tag=f"p3_sb{s}_{it}_{c4}")
                    k.act(AB[:, 4:5], zac, ACT.Ln, reads=["ABsb"], writes=["ABsb"], tag=f"p3_la{s}_{it}_{c4}")
                    k.act(AB[:, 5:6], zbc, ACT.Ln, reads=["ABsb"], writes=["ABsb"], tag=f"p3_lb{s}_{it}_{c4}")
                    k.dma("gpsimd", lac_d.ap()[s, cid][:, 0:3], AB[:, 4:7], ["ABsb"], [f"lac:{s}:{cid}"], tag=f"p3_stab{s}_{it}_{c4}")

    if "p4" in phases:
        NCH = SEQ // 128
        I128 = mask_sb[:, 3, :]
        TRIL_S = mask_sb[:, 0, :]
        TRIU_S = mask_sb[:, 1, :]
        TRIU_I = mask_sb[:, 2, :]
        ZROW = mask_sb[0:1, 0, :]

        def slot(i):
            return Hf[:, i * 128:(i + 1) * 128]

        SLOTMAP = {"KF": 0, "QF": 1, "GF": 2, "KT": 3, "VT": 4, "LAB": 5,
                   "RB": 12, "N": 13, "NT": 14, "Xa": 15, "XTa": 16, "Xb": 17, "XTb": 18,
                   "T1s": 19, "XVF": 20, "WBN": 21, "XVT": 22, "YF": 23, "L3m": 24,
                   "PTm": 25, "KB": 26, "OO": 27, "S0": 28, "S0T": 29, "COLS": 30, "Zt": 31}
        PERSEQ = 32

        def SL(s, name, par=0):
            i = SLOTMAP[name] + (6 * par if name in ("KF", "QF", "GF", "KT", "VT", "LAB") else 0)
            return slot(s * PERSEQ + i), f"q{s}s{i}"


        def ROW(s, off):
            return PROW[0:1, s * 512 + off:s * 512 + off + 128]

        def ROW2(s):
            return PROW[0:2, s * 512:s * 512 + 128]

        sch.fence_from(["Hq0", "Hq1", "Hq2", "Hq3", "A1", "A2", "BH1", "BH2",
                        "BH1p0", "BH1p1", "BH2p0", "BH2p1", "R1", "R2", "STr"],
                       [f"q{s}s{i}" for s in range(NSEQ) for i in range(PERSEQ)] +
                       [f"PROW{s}" for s in range(NSEQ)])
        for s in range(NSEQ):
            sm0, r0 = SL(s, "S0")
            sm1, r1 = SL(s, "S0T")
            k.sch.op("vector", lambda e, a=sm0: e.memset(a, 0.0), [], [r0], tag=f"p4_z0{s}")
            k.sch.op("vector", lambda e, a=sm1: e.memset(a, 0.0), [], [r1], tag=f"p4_z1{s}")

        def p4_load(s, j):
            par = j % 2
            csl = slice(j * 128, (j + 1) * 128)
            for name, dsrc, dn in (("KF", kfm_d.ap()[s][:, csl], f"kfm:{s}:{j // 4}"),
                                   ("QF", qfm_d.ap()[s][:, csl], f"qfm:{s}:{j // 4}"),
                                   ("GF", gfm_d.ap()[s][:, csl], f"gfm:{s}:{j // 4}"),
                                   ("KT", ktm_d.ap()[s, j], f"ktm:{s}:{j}"),
                                   ("VT", vtm_d.ap()[s, j], f"vtm:{s}:{j}")):
                ap, rn = SL(s, name, par)
                k.dma("sync", ap, dsrc, [rn, dn], [rn], tag=f"p4_ld{name}{s}_{j}")
            ap, rn = SL(s, "LAB", par)
            k.dma("sync", ap[:, 0:3], lac_d.ap()[s, j][:, 0:3], [rn, f"lac:{s}:{j}"], [rn], tag=f"p4_ldlab{s}_{j}")

        def p4_chunk(s, j):
            par = j % 2
            KF, rKF = SL(s, "KF", par)
            QF, rQF = SL(s, "QF", par)
            GF, rGF = SL(s, "GF", par)
            KT, rKT = SL(s, "KT", par)
            VT, rVT = SL(s, "VT", par)
            LAB, rLAB = SL(s, "LAB", par)
            RBs, rRB = SL(s, "RB")
            Nb, rN = SL(s, "N")
            NTb, rNT = SL(s, "NT")
            T1s, rT1 = SL(s, "T1s")
            XVF, rXVF = SL(s, "XVF")
            WBN, rWBN = SL(s, "WBN")
            XVT, rXVT = SL(s, "XVT")
            YF, rYF = SL(s, "YF")
            L3m, rL3 = SL(s, "L3m")
            PTm, rPT = SL(s, "PTm")
            KB, rKB = SL(s, "KB")
            Zt, rZt = SL(s, "Zt")
            OO, rOO = SL(s, "OO")
            S0, rS0 = SL(s, "S0")
            S0T, rS0T = SL(s, "S0T")
            COLS, rC = SL(s, "COLS")
            colA = COLS[:, 0:1]
            tcol = COLS[:, 1:2]
            A_col = COLS[:, 2:3]
            acai = COLS[:, 3:4]
            bacol = COLS[:, 4:5]
            Accol = COLS[:, 5:6]
            lgac = COLS[:, 6:7]
            ntcol = COLS[:, 7:8]
            ncolA = COLS[:, 8:9]
            bG = 2 + 3 * s
            bW = 3 + 3 * s
            bV = 4 + 3 * s
            rG = f"pb{bG}"
            rW = f"pb{bW}"
            rV = f"pb{bV}"
            J = f"{s}_{j}"
            la_row = ROW(s, 0)
            lb_row = ROW(s, 128)
            logA = ROW(s, 256)
            A_row = ROW(s, 384)
            rPROW = f"PROW{s}"
            k.mmgroup([(pb[bV][0:1, 0:128], LAB[:, 0:1], I128, True, True),
                       (pb[bV][0:1, 128:256], LAB[:, 1:2], I128, True, True)],
                      [rLAB, "masksb"], [rV], tag=f"p4_rows{J}")
            k.copy(la_row, pb[bV][0:1, 0:128], [rV], [rPROW], tag=f"p4_rowsev{J}")
            k.copy(lb_row, pb[bV][0:1, 128:256], [rV], [rPROW], tag=f"p4_rowsev2{J}")
            k.scan(logA, la_row, ZROW, 0.0, ALU.add, ALU.add, [rPROW, "masksb"], [rPROW], tag=f"p4_scan{J}")
            k.act(A_row, logA, ACT.Exp, reads=[rPROW], writes=[rPROW], tag=f"p4_arow{J}")
            k.mmgroup([(pb[bG][:, 0:128], KF, KF)], [rKF], [rG], tag=f"p4_G{J}")
            k.mmgroup([(pb[bV][:, 128:129], logA, one11_sb[:])], [rPROW, "one11"], [rV], tag=f"p4_colA{J}")
            k.copy(colA, pb[bV][:, 128:129], [rV], [rC], tag=f"p4_colAev{J}")
            k.mmgroup([(pb[bW][:, 0:128], ones_sb[:], logA)], [rPROW, "ones1"], [rW], tag=f"p4_RB{J}")
            k.copy(RBs, pb[bW][:, 0:128], [rW], [rRB], tag=f"p4_RBev{J}")
            k.tt(tcol, colA, LAB[:, 1:2], ALU.add, [rC, rLAB], [rC], tag=f"p4_tcol{J}")
            k.ts(ntcol, tcol, -1.0, None, ALU.mult, None, [rC], [rC], tag=f"p4_ntcol{J}")
            k.ts(ncolA, colA, -1.0, None, ALU.mult, None, [rC], [rC], tag=f"p4_ncolA{J}")
            k.ts(Nb, RBs, ntcol, None, ALU.add, None, [rRB, rC], [rN], tag=f"p4_D0{J}")
            k.ts(Nb, Nb, -1.0, 0.0, ALU.mult, ALU.min, [rN], [rN], tag=f"p4_D{J}")
            k.act(Nb, Nb, ACT.Exp, reads=[rN], writes=[rN], tag=f"p4_expD{J}")
            k.tt(Nb, Nb, TRIL_S, ALU.mult, [rN, "masksb"], [rN], tag=f"p4_Nm{J}")
            k.tt(Nb, Nb, pb[bG][:, 0:128], ALU.mult, [rN, rG], [rN], tag=f"p4_NG{J}")
            k.tt(lb_row, lb_row, logA, ALU.add, [rPROW], [rPROW], tag=f"p4_lab{J}")
            k.mmgroup([(pb[bW][:, 128:256], ones_sb[:], lb_row)], [rPROW, "ones1"], [rW], tag=f"p4_RB2{J}")
            k.ts(NTb, pb[bW][:, 128:256], ncolA, None, ALU.add, None, [rW, rC], [rNT], tag=f"p4_D20{J}")
            k.ts(NTb, NTb, 0.0, None, ALU.min, None, [rNT], [rNT], tag=f"p4_D2{J}")
            k.act(NTb, NTb, ACT.Exp, reads=[rNT], writes=[rNT], tag=f"p4_expD2{J}")
            k.tt(NTb, NTb, TRIU_S, ALU.mult, [rNT, "masksb"], [rNT], tag=f"p4_NTm{J}")
            k.tt(NTb, NTb, pb[bG][:, 0:128], ALU.mult, [rNT, rG], [rNT], tag=f"p4_NTG{J}")
            Xc, rXc = SL(s, "Xa")
            XTc, rXTc = SL(s, "XTa")
            Xo, rXo = SL(s, "Xb")
            XTo, rXTo = SL(s, "XTb")
            k.tt(Xc, I128, Nb, ALU.subtract, ["masksb", rN], [rXc], tag=f"p4_X0{J}")
            k.tt(XTc, I128, NTb, ALU.subtract, ["masksb", rNT], [rXTc], tag=f"p4_XT0{J}")
            for itn in range(6):
                k.mmgroup([(pb[bW][:, 256:384], NTb, Xc, True, False),
                           (pb[bW][:, 256:384], I128, Xc, False, True)],
                          [rNT, rXc, "masksb"], [rW], tag=f"p4_T1{J}_{itn}")
                k.copy(T1s, pb[bW][:, 256:384], [rW], [rT1], tag=f"p4_T1ev{J}_{itn}")
                k.mmgroup([(pb[bW][:, 384:512], XTc, T1s)], [rXTc, rT1], [rW], tag=f"p4_X2{J}_{itn}")
                k.stt(Xo, Xc, 2.0, pb[bW][:, 384:512], ALU.mult, ALU.subtract, [rXc, rW], [rXo], tag=f"p4_Xn{J}_{itn}")
                k.mmgroup([(pb[bW][:, 256:384], T1s, XTc)], [rT1, rXTc], [rW], tag=f"p4_XT2{J}_{itn}")
                k.stt(XTo, XTc, 2.0, pb[bW][:, 256:384], ALU.mult, ALU.subtract, [rXTc, rW], [rXTo], tag=f"p4_XTn{J}_{itn}")
                Xc, Xo = Xo, Xc
                rXc, rXo = rXo, rXc
                XTc, XTo = XTo, XTc
                rXTc, rXTo = rXTo, rXTc
            k.act(A_col, colA, ACT.Exp, reads=[rC], writes=[rC], tag=f"p4_acol{J}")
            k.tt(bacol, LAB[:, 2:3], A_col, ALU.mult, [rLAB, rC], [rC], tag=f"p4_bacol{J}")
            k.mmgroup([(pb[bV][:, 136:137], ones_sb[:], logA[0:1, 127:128]),
                       (pb[bV][:, 137:138], ones_sb[:], A_row[0:1, 127:128])],
                      [rPROW, "ones1"], [rV], tag=f"p4_accol{J}")
            k.copy(lgac, pb[bV][:, 136:137], [rV], [rC], tag=f"p4_lgacev{J}")
            k.copy(Accol, pb[bV][:, 137:138], [rV], [rC], tag=f"p4_accolev{J}")
            k.tt(lgac, lgac, colA, ALU.subtract, [rC], [rC], tag=f"p4_acai0{J}")
            k.act(acai, lgac, ACT.Exp, reads=[rC], writes=[rC], tag=f"p4_acai{J}")
            k.ts(VT, VT, LAB[:, 2:3], None, ALU.mult, None, [rVT, rLAB], [rVT], tag=f"p4_Vb{J}")
            k.ts(KB, KT, bacol, None, ALU.mult, None, [rKT, rC], [rKB], tag=f"p4_Kba{J}")
            k.mmgroup([(pb[bW][:, 0:128], VT, XTc)], [rVT, rXTc], [rW], tag=f"p4_XVF{J}")
            k.copy(XVF, pb[bW][:, 0:128], [rW], [rXVF], tag=f"p4_XVFev{J}")
            k.mmgroup([(pb[bW][:, 128:256], KB, XTc)], [rKB, rXTc], [rW], tag=f"p4_WBF{J}")
            k.ts(WBN, pb[bW][:, 128:256], -1.0, None, ALU.mult, None, [rW], [rWBN], tag=f"p4_WBN{J}")
            k.mmgroup([(pb[bW][:, 256:384], XTc, VT)], [rXTc, rVT], [rW], tag=f"p4_XVT{J}")
            k.copy(XVT, pb[bW][:, 256:384], [rW], [rXVT], tag=f"p4_XVTev{J}")
            k.mmgroup([(pb[bG][:, 128:256], S0, QF)], [rS0, rQF], [rG], tag=f"p4_YF{J}")
            k.copy(YF, pb[bG][:, 128:256], [rG], [rYF], tag=f"p4_YFev{J}")
            k.mmgroup([(pb[bW][:, 384:512], XVF, QF, True, False),
                       (pb[bW][:, 384:512], WBN, YF, False, True)],
                      [rXVF, rWBN, rQF, rYF], [rW], tag=f"p4_PT{J}")
            k.ts(L3m, RBs, ncolA, None, ALU.add, None, [rRB, rC], [rL3], tag=f"p4_D30{J}")
            k.ts(L3m, L3m, 0.0, None, ALU.min, None, [rL3], [rL3], tag=f"p4_D3{J}")
            k.act(L3m, L3m, ACT.Exp, reads=[rL3], writes=[rL3], tag=f"p4_expD3{J}")
            k.tt(L3m, L3m, TRIU_I, ALU.mult, [rL3, "masksb"], [rL3], tag=f"p4_L3mm{J}")
            k.tt(PTm, pb[bW][:, 384:512], L3m, ALU.mult, [rW, rL3], [rPT], tag=f"p4_PTm{J}")
            k.mmgroup([(pb[bG][:, 256:384], KT, PTm)], [rKT, rPT], [rG], tag=f"p4_O2{J}")
            k.mmgroup([(pb[bV][:, 256:384], ones_sb[:], A_row)], [rPROW, "ones1"], [rV], tag=f"p4_ABB{J}")
            k.tt(OO, YF, pb[bV][:, 256:384], ALU.mult, [rYF, rV], [rOO], tag=f"p4_Ys{J}")
            k.tt(OO, OO, pb[bG][:, 256:384], ALU.add, [rOO, rG], [rOO], tag=f"p4_O{J}")
            k.tt(OO, OO, GF, ALU.mult, [rOO, rGF], [rOO], tag=f"p4_Og{J}")
            k.dma("gpsimd", ofm_d.ap()[s][:, j * 128:(j + 1) * 128], OO, [rOO], [f"ofm:{s}:{j}"], tag=f"p4_stO{J}")
            k.mmgroup([(pb[bW][:, 0:128], WBN, S0T)], [rWBN, rS0T], [rW], tag=f"p4_WS{J}")
            k.tt(Zt, XVT, pb[bW][:, 0:128], ALU.add, [rXVT, rW], [rZt], tag=f"p4_Zt{J}")
            k.ts(Zt, Zt, acai, None, ALU.mult, None, [rZt, rC], [rZt], tag=f"p4_Z{J}")
            k.mmgroup([(pb[bW][:, 128:256], Zt, KT)], [rZt, rKT], [rW], tag=f"p4_S0n{J}")
            k.mmgroup([(pb[bW][:, 256:384], KT, Zt)], [rKT, rZt], [rW], tag=f"p4_S0Tn{J}")
            k.stt(S0, S0, Accol, pb[bW][:, 128:256], ALU.mult, ALU.add, [rS0, rC, rW], [rS0], tag=f"p4_S0{J}")
            k.stt(S0T, S0T, Accol, pb[bW][:, 256:384], ALU.mult, ALU.add, [rS0T, rC, rW], [rS0T], tag=f"p4_S0T{J}")
            if s == 0 and j == 0 and "p4dbg" in dbg:
                pd = k.dram_t("p4dump", [8, 128, 128], F32, kind="ExternalOutput")
                k.dma("gpsimd", pd.ap()[0][:, 0:4], LAB[:, 0:4], [rLAB], [], tag="p4d_lab")
                k.dma("gpsimd", pd.ap()[1][0:2, :], PROW[0:2, 0:128], [rPROW], [], tag="p4d_rows0")
                k.dma("gpsimd", pd.ap()[1][2:3, :], PROW[0:1, 256:384], [rPROW], [], tag="p4d_loga")
                k.dma("gpsimd", pd.ap()[1][3:4, :], PROW[0:1, 384:512], [rPROW], [], tag="p4d_arow")
                k.dma("gpsimd", pd.ap()[2], Nb, [rN], [], tag="p4d_N")
                k.dma("gpsimd", pd.ap()[3], NTb, [rNT], [], tag="p4d_NT")
                k.dma("gpsimd", pd.ap()[4], XTc, [rXTc], [], tag="p4d_XT")
                k.dma("gpsimd", pd.ap()[5], PTm, [rPT], [], tag="p4d_PTm")
                k.dma("gpsimd", pd.ap()[6], OO, [rOO], [], tag="p4d_OO")
                k.dma("gpsimd", pd.ap()[7][:, 0:8], COLS[:, 0:8], [rC], [], tag="p4d_cols")

        p4_load(0, 0)
        p4_load(1, 0)
        for j in range(NCH):
            for s in range(NSEQ):
                if j + 1 < NCH:
                    p4_load(s, j + 1)
                p4_chunk(s, j)

    if "p5" in phases:
        sch.fence_from([f"q{s}s{i}" for s in range(NSEQ) for i in range(32)],
                       ["Hq0", "Hq1", "Hq2", "Hq3"])
        sch.fence_from(["PROW0", "PROW1"], ["PROW0"])
        WB8 = WB[:].bitcast(FP8)
        w1v = WB8[:, 0:NFC * HFF].rearrange("p (a b) -> p a b", a=NFC)
        w2v = WB8[:, NFC * HFF:2 * NFC * HFF].rearrange("p (a b) -> p a b", a=NHC)
        k.dma("sync", w1v, w1q_d.ap().rearrange("(a p) b -> p a b", p=128),
              ["WB0", "WB1", "d_qw1"], ["WB0", "WB1"], tag="p5_ldw1")
        k.dma("sync", w2v, w2q_d.ap().rearrange("(a p) b -> p a b", p=128),
              ["WB2", "WB3", "d_qw2"], ["WB2", "WB3"], tag="p5_ldw2")
        WSv = WS[:, :, 0:1024]
        TILES = [(s, it) for s in range(NSEQ) for it in range(NT1)]

        def p5_headA(ti):
            s, it = TILES[ti]
            t0 = it * T1
            sl = 2 * (ti % 2)
            ofns = [f"ofm:{s}:{it * 4 + c}" for c in range(4)]
            osb = H[:, 0, 0:T1]
            k.dma("sync", osb, ofm_d.ap()[s][:, t0:t0 + T1], ["Hq0"] + ofns, ["Hq0"], tag=f"p5_ldo{s}_{it}")
            k.dma("sync", WSv[:, sl, :], wl1osp.ap()[0].rearrange("(a p) b -> p a b", p=128)[:, 0, :],
                  [f"WSa{sl}", "d_l1wo"], [f"WSa{sl}"], tag=f"p5_ldwo0{s}_{it}")
            k.dma("sync", WSv[:, sl + 1, :], wl1osp.ap()[1].rearrange("(a p) b -> p a b", p=128)[:, 0, :],
                  [f"WSa{sl + 1}", "d_l1wo"], [f"WSa{sl + 1}"], tag=f"p5_ldwo1{s}_{it}")
            k.copy(BH1[:, 0, 0:T1], osb, ["Hq0"], ["BH1p0"], tag=f"p5_ohi{s}_{it}")
            k.tt(BH1[:, 1, 0:T1], osb, BH1[:, 0, 0:T1], ALU.subtract, ["Hq0", "BH1p0"], ["BH1p0"], tag=f"p5_olo{s}_{it}")
            for oc in range(NFC):
                bank = 2 + oc % 4
                lh = WSv[:, sl, oc * 128:(oc + 1) * 128]
                ll = WSv[:, sl + 1, oc * 128:(oc + 1) * 128]
                mms = [(pb[bank][:, 0:T1], lh, BH1[:, 0, 0:T1]),
                       (pb[bank][:, 0:T1], lh, BH1[:, 1, 0:T1]),
                       (pb[bank][:, 0:T1], ll, BH1[:, 0, 0:T1])]
                k.mmgroup(mms, ["BH1p0", f"WSa{sl}", f"WSa{sl + 1}"], [f"pb{bank}"], tag=f"p5_omm{s}_{it}_{oc}")
                k.act(H[:, 8 + oc, 0:T1], pb[bank][:, 0:T1], ACT.Identity, bias=bias_sb[:, 80 + oc:81 + oc],
                      reads=[f"pb{bank}", "biassb"], writes=["Hq1"], tag=f"p5_oev{s}_{it}_{oc}")

        def p5_headB(ti):
            s, it = TILES[ti]
            t0 = it * T1
            x2n = f"x2:{s}:{it}"
            x3n = f"x3:{s}:{it}"
            xon = f"xo:{s}:{it}"
            emit_ln(H[:, 8:16, 0:T1], NFC, T1, ["Hq1"], f"p5ln1{s}_{it}", H[:, 24:32, 0:T1], "Hq3",
                    xn=H[:, 8:16, 0:T1], xnres="Hq1", psA=6, psB=7, par=1)

        def p5_body(ti):
            s, it = TILES[ti]
            t0 = it * T1
            x2n = f"x2:{s}:{it}"
            xt = A1[:, :, 0:T1]
            k.dma("sync", xt, x2_d.ap()[s, :, :, t0:t0 + T1].transpose([1, 0, 2]), ["A1", x2n], ["A1"], tag=f"p5_ldx{s}_{it}")
            k.dma("sync", BH2[:, :, 0:T1], dmask_in.ap()[s, :, :, t0:t0 + T1].transpose([1, 0, 2]),
                  ["BH2p0", "BH2p1"], ["BH2p0", "BH2p1"], tag=f"p5_ldm{s}_{it}")
            k.tt(xt, xt, H[:, 8:16, 0:T1], ALU.add, ["A1", "Hq1"], ["A1"], tag=f"p5_res1{s}_{it}")
            x3n = f"x3:{s}:{it}"
            xon = f"xo:{s}:{it}"
            k.dma("gpsimd", x3_d.ap()[s, :, :, t0:t0 + T1].transpose([1, 0, 2]), xt,
                  ["A1", x3n], [x3n], tag=f"p5_st3{s}_{it}")
            k.dma("scalar", xout.ap()[s, :, :, t0:t0 + T1], x3_d.ap()[s, :, :, t0:t0 + T1],
                  [xon, x3n], [xon], tag=f"p5_pco{s}_{it}")
            emit_ln(xt, NFC, T1, ["A1"], f"p5ln2{s}_{it}", A2[:, :, 0:T1], "A2",
                    xn=H[:, 16:24, 0:T1], xnres="Hq2", psA=6, psB=7, par=1)
            xn3 = H[:, 16:24, 0:T1]
            t1q = A2[:, :, 0:T1]
            k.act(t1q, xn3, ACT.Abs, reads=["Hq2"], writes=["A2"], tag=f"p5_qa{s}_{it}")
            k.act(t1q, t1q, ACT.Ln, bias=ccol("tiny"), reads=["A2", "ccsb"], writes=["A2"], tag=f"p5_ql{s}_{it}")
            k.ts(t1q, t1q, 1.0 / LN2, -0.5, ALU.mult, ALU.add, ["A2"], ["A2"], tag=f"p5_qv{s}_{it}")
            k.ts(t1q, t1q, dcol("bcol"), dcol("mbcol"), ALU.add, ALU.add, ["A2", "dcsb"], ["A2"], tag=f"p5_qe{s}_{it}")
            k.ts(t1q, t1q, -6.0, 8.0, ALU.max, ALU.min, ["A2"], ["A2"], tag=f"p5_qc{s}_{it}")
            k.act(t1q, t1q, ACT.Exp, bias=dcol("bs4"), scale=LN2, reads=["A2", "dcsb"], writes=["A2"], tag=f"p5_qb{s}_{it}")
            k.tt(xn3, xn3, t1q, ALU.add, ["Hq2", "A2"], ["Hq2"], tag=f"p5_qu{s}_{it}")
            k.tt(BQ8[:, 0, :, 0:T1], xn3, t1q, ALU.subtract, ["Hq2", "A2"], ["BQ8a"], tag=f"p5_qq{s}_{it}")
            for hc in range(NHC):
                bank = 2 + hc % 2
                mms = [(pb[bank][:, 0:T1], w1v[:, kc, hc * 128:(hc + 1) * 128], BQ8[:, 0, kc, 0:T1])
                       for kc in range(NFC)]
                k.mmgroup(mms, ["BQ8a", "WB0"], [f"pb{bank}"], tag=f"p5_hmm{s}_{it}_{hc}")
                k.act(H[:, hc, 0:T1], pb[bank][:, 0:T1], ACT.Identity, bias=bias_sb[:, 32 + hc:33 + hc],
                      reads=[f"pb{bank}", "biassb"], writes=[f"Hq{hc // 8}"], tag=f"p5_hev{s}_{it}_{hc}")
            for gg in range(4):
                csl = slice(gg * 8, (gg + 1) * 8)
                hq = ["Hq0", "Hq1", "Hq2", "Hq3"][gg]
                Hg = H[:, csl, 0:T1]
                srn = "A2" if gg % 2 == 0 else "A1"
                t1_ = (A2 if gg % 2 == 0 else A1)[:, :, 0:T1]
                k.act(t1_, Hg, ACT.Abs, reads=[hq], writes=[srn], tag=f"p5_h1a{s}_{it}_{gg}")
                k.act(t1_, t1_, ACT.Ln, bias=ccol("tiny"), reads=[srn, "ccsb"], writes=[srn], tag=f"p5_h1l{s}_{it}_{gg}")
                k.ts(t1_, t1_, 1.0 / LN2, -0.5, ALU.mult, ALU.add, [srn], [srn], tag=f"p5_h1v{s}_{it}_{gg}")
                k.ts(t1_, t1_, dcol("bcol"), dcol("mbcol"), ALU.add, ALU.add, [srn, "dcsb"], [srn], tag=f"p5_h1e{s}_{it}_{gg}")
                k.ts(t1_, t1_, -14.0, 8.0, ALU.max, ALU.min, [srn], [srn], tag=f"p5_h1c{s}_{it}_{gg}")
                k.act(t1_, t1_, ACT.Exp, bias=dcol("bs5"), scale=LN2, reads=[srn, "dcsb"], writes=[srn], tag=f"p5_h1b{s}_{it}_{gg}")
                k.stt(Hg, Hg, 0.0, t1_, ALU.max, ALU.add, [hq, srn], [hq], tag=f"p5_h1u{s}_{it}_{gg}")
                k.tt(Hg, Hg, t1_, ALU.subtract, [hq, srn], [hq], tag=f"p5_h1q{s}_{it}_{gg}")
            for gg in range(4):
                par = gg % 2
                csl = slice(gg * 8, (gg + 1) * 8)
                hq = ["Hq0", "Hq1", "Hq2", "Hq3"][gg]
                BQ = BQ8[:, 0, :, :] if par == 0 else BQ8[:, 1, :, :]
                bq_r = ["BQ8a"] if par == 0 else ["BQ8b"]
                Hg = H[:, csl, 0:T1]
                srn = "A2" if gg % 2 == 0 else "A1"
                t1_ = (A2 if gg % 2 == 0 else A1)[:, :, 0:T1]
                k.act(t1_, Hg, ACT.Ln, bias=ccol("tiny"), reads=[hq, "ccsb"], writes=[srn], tag=f"p5_h2l{s}_{it}_{gg}")
                k.ts(t1_, t1_, 1.0 / LN2, -0.5, ALU.mult, ALU.add, [srn], [srn], tag=f"p5_h2v{s}_{it}_{gg}")
                k.ts(t1_, t1_, dcol("bcol"), dcol("mbcol"), ALU.add, ALU.add, [srn, "dcsb"], [srn], tag=f"p5_h2e{s}_{it}_{gg}")
                k.ts(t1_, t1_, -6.0, 8.0, ALU.max, ALU.min, [srn], [srn], tag=f"p5_h2c{s}_{it}_{gg}")
                k.act(t1_, t1_, ACT.Exp, bias=ccol("b8ln2"), scale=-LN2, reads=[srn, "ccsb"], writes=[srn], tag=f"p5_h2p{s}_{it}_{gg}")
                k.ts(t1_, t1_, dcol("bcol"), dcol("mbcol"), ALU.add, ALU.add, [srn, "dcsb"], [srn], tag=f"p5_h2pe{s}_{it}_{gg}")
                k.stt(Hg, Hg, 2.0 ** -5, t1_, ALU.mult, ALU.mult, [hq, srn], [hq], tag=f"p5_h2t{s}_{it}_{gg}")
                k.ts(Hg, Hg, dcol("bcol"), dcol("mbcol"), ALU.add, ALU.add, [hq, "dcsb"], [hq], tag=f"p5_h2n{s}_{it}_{gg}")
                k.act(t1_, t1_, ACT.Ln, reads=[srn], writes=[srn], tag=f"p5_h2rl{s}_{it}_{gg}")
                k.ts(t1_, t1_, -1.0, 14.0 * LN2, ALU.mult, ALU.add, [srn], [srn], tag=f"p5_h2rn{s}_{it}_{gg}")
                k.act(t1_, t1_, ACT.Exp, reads=[srn], writes=[srn], tag=f"p5_h2rx{s}_{it}_{gg}")
                k.ts(t1_, t1_, dcol("bcol"), dcol("mbcol"), ALU.add, ALU.add, [srn, "dcsb"], [srn], tag=f"p5_h2se{s}_{it}_{gg}")
                k.stt(BQ[:, :, 0:T1], Hg, 2.0 ** -9, t1_, ALU.mult, ALU.mult, [hq, srn], bq_r, tag=f"p5_h2q{s}_{it}_{gg}")
                for fc in range(NFC):
                    hc = gg * 8 + fc
                    mms = []
                    for dc in range(NFC):
                        wp = w2v[:, hc, dc * 128:(dc + 1) * 128]
                        mms.append((pb[dc][:, 0:T1], wp, BQ[:, fc, 0:T1], hc == 0, hc == NHC - 1))
                    k.mmgroup(mms, bq_r + ["WB2"],
                              [f"pb{dc2}" for dc2 in range(NFC)], tag=f"p5_ymm{s}_{it}_{hc}")
            for dc in range(NFC):
                k.act(A2[:, dc, 0:T1], pb[dc][:, 0:T1], ACT.Identity, bias=bias_sb[:, 64 + dc:65 + dc],
                      reads=[f"pb{dc}", "biassb"], writes=["A2"], tag=f"p5_yev{s}_{it}_{dc}")

        def p5_tailA(ti):
            s, it = TILES[ti]
            t1y = H[:, 16:24, 0:T1]
            k.act(t1y, A2[:, :, 0:T1], ACT.Abs, reads=["A2"], writes=["Hq2"], tag=f"p5_y1a{s}_{it}")
            k.act(t1y, t1y, ACT.Ln, bias=ccol("tiny"), reads=["Hq2", "ccsb"], writes=["Hq2"], tag=f"p5_y1l{s}_{it}")
            k.ts(t1y, t1y, 1.0 / LN2, -0.5, ALU.mult, ALU.add, ["Hq2"], ["Hq2"], tag=f"p5_y1v{s}_{it}")
            k.ts(t1y, t1y, dcol("bcol"), dcol("mbcol"), ALU.add, ALU.add, ["Hq2", "dcsb"], ["Hq2"], tag=f"p5_y1e{s}_{it}")
            k.ts(t1y, t1y, -14.0, 8.0, ALU.max, ALU.min, ["Hq2"], ["Hq2"], tag=f"p5_y1c{s}_{it}")
            k.act(t1y, t1y, ACT.Exp, bias=dcol("bs5"), scale=LN2, reads=["Hq2", "dcsb"], writes=["Hq2"], tag=f"p5_y1b{s}_{it}")
            k.tt(A2[:, :, 0:T1], A2[:, :, 0:T1], t1y, ALU.add, ["A2", "Hq2"], ["A2"], tag=f"p5_y1u{s}_{it}")
            k.tt(A2[:, :, 0:T1], A2[:, :, 0:T1], t1y, ALU.subtract, ["A2", "Hq2"], ["A2"], tag=f"p5_y1q{s}_{it}")

        def p5_tailB(ti):
            s, it = TILES[ti]
            t0 = it * T1
            xon = f"xo:{s}:{it}"
            k.copy(A1[:, :, 0:T1], BH2[:, :, 0:T1], ["BH2p0", "BH2p1"], ["A1"], tag=f"p5_mcp{s}_{it}")
            k.tt(A2[:, :, 0:T1], A2[:, :, 0:T1], A1[:, :, 0:T1], ALU.mult, ["A2", "A1"], ["A2"], tag=f"p5_msk{s}_{it}")
            emit_ln(A2[:, :, 0:T1], NFC, T1, ["A2"], f"p5ln3{s}_{it}", H[:, 16:24, 0:T1], "Hq2",
                    xn=A2[:, :, 0:T1], xnres="A2", psA=0, psB=1, par=0)
            k.sch.op("gpsimd", lambda e, s=s, t0=t0: e.dma_start(
                out=xout.ap()[s, :, :, t0:t0 + T1].transpose([1, 0, 2]),
                in_=A2[:, :, 0:T1], accum_op=ALU.add),
                ["A2", xon], [xon], dma=True, tag=f"p5_sto{s}_{it}")

        NTILES = len(TILES)
        p5_headA(0)
        p5_headB(0)
        for ti in range(NTILES):
            p5_body(ti)
            if ti + 1 < NTILES:
                p5_headA(ti + 1)
            p5_tailA(ti)
            if ti + 1 < NTILES:
                p5_headB(ti + 1)
            p5_tailB(ti)

    if "kvdump" in dbg:
        kv_dump = k.dram_t("kvdump", [128, 4 * NFC], F32, kind="ExternalOutput")
        dn = k.dma("gpsimd", kv_dump.ap(), accsb[:, 0:4 * NFC], ["accsb"], [], tag="kvdump")
        sch.barrier_end(dn)

    if "copyout" in phases:
        for s2 in range(NSEQ):
            dn = k.dma("gpsimd", xout.ap()[s2], x_in.ap()[s2], [], [], tag=f"cpout{s2}")
            sch.barrier_end(dn)

    with nc.Block() as block:
        sch.finalize(block)
    stack.close()
    return nc


def host_masks():
    return np.stack([np.tril(np.ones((128, 128), np.float32), -1),
                     np.triu(np.ones((128, 128), np.float32), 1),
                     np.triu(np.ones((128, 128), np.float32), 0),
                     np.eye(128, dtype=np.float32)])


BIAS = {"bq0": 0, "bk0": 8, "bv0": 16, "bo0": 24, "b1": 32, "b2": 64,
        "bq1": 72, "bk1": 73, "bv1": 74, "bg1": 75, "ba1": 76, "bb1": 77,
        "bo1": 80, "lnw": 88, "lnb": 120}


def host_biases(inp):
    bias = np.zeros((128, 160), np.float32)
    bias[:, 0:8] = np.asarray(inp["L0_attn_bq"]).reshape(8, 128).T
    bias[:, 8:16] = np.asarray(inp["L0_attn_bk"]).reshape(8, 128).T
    bias[:, 16:24] = np.asarray(inp["L0_attn_bv"]).reshape(8, 128).T
    bias[:, 24:32] = np.asarray(inp["L0_attn_bo"]).reshape(8, 128).T
    bias[:, 32:64] = np.asarray(inp["L1_ffn_b1"]).reshape(32, 128).T
    bias[:, 64:72] = np.asarray(inp["L1_ffn_b2"]).reshape(8, 128).T
    bias[:, 72:73] = np.asarray(inp["L1_attn_bq"]).reshape(1, 128).T
    bias[:, 73:74] = np.asarray(inp["L1_attn_bk"]).reshape(1, 128).T
    bias[:, 74:75] = np.asarray(inp["L1_attn_bv"]).reshape(1, 128).T
    bias[:, 75:76] = np.asarray(inp["L1_attn_bg"]).reshape(1, 128).T
    bias[:, 76] = float(np.asarray(inp["L1_attn_ba"])[0])
    bias[:, 77] = float(np.asarray(inp["L1_attn_bb"])[0])
    bias[0, 78] = float(np.asarray(inp["L1_attn_ba"])[0])
    bias[1, 78] = float(np.asarray(inp["L1_attn_bb"])[0])
    bias[:, 80:88] = np.asarray(inp["L1_attn_bo"]).reshape(8, 128).T
    bias[:, 88:120] = np.asarray(inp["L0_ffn_ln_w"]).reshape(32, 128).T
    bias[:, 120:152] = np.asarray(inp["L0_ffn_ln_b"]).reshape(32, 128).T
    return bias


def _fm(x):
    return np.ascontiguousarray(np.asarray(x).T.reshape(NFC, 128, x.shape[0]))


def host_inputs(inp, phases):
    import ml_dtypes
    cc = host_consts()
    masks = host_masks()
    biases = host_biases(inp)
    ones_row = np.ones((1, 128), np.float32)
    one11 = np.ones((1, 1), np.float32)
    x = np.asarray(inp["L0_attn_x"])
    w_l0qkv = np.stack([np.asarray(inp["L0_attn_wq"]).reshape(D, D),
                        np.asarray(inp["L0_attn_wk"]).reshape(D, D),
                        np.asarray(inp["L0_attn_wv"]).reshape(D, D)]).astype(np.float32)
    w_l0o = np.ascontiguousarray(np.asarray(inp["L0_attn_wo"]).reshape(D, D))
    w_fcT = np.ascontiguousarray(np.asarray(inp["L0_ffn_W_fc"]).T)
    w_pjT = np.ascontiguousarray(np.asarray(inp["L0_ffn_W_proj"]).T)
    wab = np.zeros((D, C1), np.float32)
    wab[:, 0:1] = np.asarray(inp["L1_attn_Wa"])
    wab[:, 1:2] = np.asarray(inp["L1_attn_Wb"])
    w_l1p = np.stack([np.asarray(inp["L1_attn_Wq"]), np.asarray(inp["L1_attn_Wk"]),
                      np.asarray(inp["L1_attn_Wv"]), np.asarray(inp["L1_attn_Wg"]), wab]).astype(np.float32)
    w_l1o = np.ascontiguousarray(np.asarray(inp["L1_attn_Wo"]))
    w_1T = np.ascontiguousarray(np.asarray(inp["L1_ffn_w1"]).T)
    w_2T = np.ascontiguousarray(np.asarray(inp["L1_ffn_w2"]).T)
    import jax
    _pf = jax.config.jax_threefry_partitionable
    jax.config.update("jax_threefry_partitionable", True)
    try:
        with jax.default_device(jax.devices("cpu")[0]):
            keep = np.asarray(jax.random.bernoulli(
                jax.random.key(42, impl="threefry2x32"), 0.8, (B, SEQ, D)))
    finally:
        jax.config.update("jax_threefry_partitionable", _pf)
    dmask_full = np.where(keep, np.float32(1.25), np.float32(0.0))
    in_maps = []
    for core in range(8):
        s0 = 2 * core
        xs = np.stack([_fm(x[s0]), _fm(x[s0 + 1])])
        dmc = np.stack([_fm(dmask_full[s0]), _fm(dmask_full[s0 + 1])]).astype(ml_dtypes.bfloat16)
        in_maps.append({
            "x": xs, "cc": cc, "ones_row": ones_row, "one_11": one11,
            "masks": masks, "biases": biases, "dmask": dmc,
            "w_l0qkv": w_l0qkv, "w_l0o": w_l0o, "w_fcT": w_fcT, "w_pjT": w_pjT,
            "w_l1p": w_l1p, "w_l1o": w_l1o, "w_1T": w_1T, "w_2T": w_2T,
        })
    return in_maps


ALL_PHASES = ["p0w", "p0t", "p0q", "p1", "p2a", "p2b", "p3", "p4", "p5"]


def kernel(**inputs):
    inp = {kk: np.asarray(v) for kk, v in inputs.items()}
    nc = build_program(ALL_PHASES)
    in_maps = host_inputs(inp, ALL_PHASES)
    res = run_bass_kernel_spmd(nc, in_maps, core_ids=list(range(8)))
    out = np.empty((B, SEQ, D), np.float32)
    for core in range(8):
        xo = np.asarray(res.results[core]["xout"])
        for s in range(NSEQ):
            out[2 * core + s] = xo[s].reshape(D, SEQ).T
    return out
